# version 12
# baseline (speedup 1.0000x reference)
import numpy as np
from contextlib import ExitStack

import concourse.bass as bass
import concourse.mybir as mybir
from concourse.bass_utils import run_bass_kernel_spmd

F32 = mybir.dt.float32
BF16 = mybir.dt.bfloat16
AF = mybir.ActivationFunctionType
ALU = mybir.AluOpType

TOK = 1024
NCH = 32
MT12 = 64
NT3 = 32
KVT = 8
NPACK = 64
ISD = 0.08838834764831845
EPS = 1e-5


class Sem:
    def __init__(self, stack, nc, name):
        self.h = stack.enter_context(nc.semaphore(name))
        self.base = 0

    def at(self, k):
        return self.base + k

    def adv(self, n):
        self.base += n


_QKV_NROPE = None


def build_program(upto=None):
    nc = bass.Bass("TRN2", target_bir_lowering=False)

    xT = nc.dram_tensor("xT", [4096, TOK], F32, kind="ExternalInput").ap()
    wq_t = nc.dram_tensor("wq_t", [NT3, 128, NCH, 128], F32, kind="ExternalInput").ap()
    wk_t = nc.dram_tensor("wk_t", [KVT, 128, NCH, 128], F32, kind="ExternalInput").ap()
    wv_t = nc.dram_tensor("wv_t", [KVT, 128, NCH, 128], F32, kind="ExternalInput").ap()
    wo_t = nc.dram_tensor("wo_t", [NT3, 128, NCH, 128], F32, kind="ExternalInput").ap()
    w12_t = nc.dram_tensor("w12_t", [MT12, 128, NCH, 128], F32, kind="ExternalInput").ap()
    w3_t = nc.dram_tensor("w3_t", [NT3, 128, NCH, 128], F32, kind="ExternalInput").ap()
    b12T = nc.dram_tensor("b12T", [128, 64], F32, kind="ExternalInput").ap()
    b3T = nc.dram_tensor("b3T", [128, 32], F32, kind="ExternalInput").ap()
    sinT = nc.dram_tensor("sinT", [128, TOK], F32, kind="ExternalInput").ap()
    cosT = nc.dram_tensor("cosT", [128, TOK], F32, kind="ExternalInput").ap()
    maskA = nc.dram_tensor("maskA", [17, 128], F32, kind="ExternalInput").ap()
    maskB = nc.dram_tensor("maskB", [17, 512], F32, kind="ExternalInput").ap()
    idem = nc.dram_tensor("idem", [128, 128], F32, kind="ExternalInput").ap()
    outT = nc.dram_tensor("outT", [4096, TOK], F32, kind="ExternalOutput").ap()
    x1T = nc.dram_tensor("x1T", [4096, TOK], F32).ap()
    x2T = nc.dram_tensor("x2T", [4096, TOK], F32).ap()

    st = ExitStack()
    with st:
        b12sb = st.enter_context(nc.sbuf_tensor("b12sb", [128, 64], F32))
        b3sb = st.enter_context(nc.sbuf_tensor("b3sb", [128, 32], F32))
        b3h = st.enter_context(nc.sbuf_tensor("b3h", [128, 32], F32))
        mA_bf = st.enter_context(nc.sbuf_tensor("mA_bf", [17, 128], BF16))
        mB_bf = st.enter_context(nc.sbuf_tensor("mB_bf", [17, 512], BF16))
        ide_bf = st.enter_context(nc.sbuf_tensor("ide_bf", [128, 128], BF16))
        ones_bf = st.enter_context(nc.sbuf_tensor("ones_bf", [128, 1], BF16))
        onesK1 = st.enter_context(nc.sbuf_tensor("onesK1", [1, 128], BF16))
        onesK1f = st.enter_context(nc.sbuf_tensor("onesK1f", [1, 128], F32))

        bar = Sem(st, nc, "bar")
        nbar = [0]
        p0s = Sem(st, nc, "p0s")
        p0g = Sem(st, nc, "p0g")
        p0v = Sem(st, nc, "p0v")
        p0c = Sem(st, nc, "p0c")
        wld = [Sem(st, nc, "wld0"), Sem(st, nc, "wld1")]
        lnx = [Sem(st, nc, f"lnx{i}") for i in range(4)]
        lns = Sem(st, nc, "lns")
        lnmm = Sem(st, nc, "lnmm")
        lnvs = Sem(st, nc, "lnvs")
        lnve = Sem(st, nc, "lnve")
        lnss = Sem(st, nc, "lnss")
        lnrs = Sem(st, nc, "lnrs")
        lnbc = Sem(st, nc, "lnbc")
        lnsb = Sem(st, nc, "lnsb")
        lnafb = Sem(st, nc, "lnafb")
        lnafw = Sem(st, nc, "lnafw")
        mmA = Sem(st, nc, "mmA")
        mmB = Sem(st, nc, "mmB")
        sg = Sem(st, nc, "sg")
        vx2 = Sem(st, nc, "vx2")
        vx4 = Sem(st, nc, "vx4")
        xld = [Sem(st, nc, "xld0"), Sem(st, nc, "xld1")]
        xst = [Sem(st, nc, "xst0"), Sem(st, nc, "xst1")]
        vdone = Sem(st, nc, "vdone")
        mm3 = Sem(st, nc, "mm3")
        sf = Sem(st, nc, "sf")
        mmq = Sem(st, nc, "mmq")
        sq = Sem(st, nc, "sq")
        scs = Sem(st, nc, "scs")
        pfree = Sem(st, nc, "pfree")
        prot = Sem(st, nc, "prot")
        pq = Sem(st, nc, "pq")
        mmsc = Sem(st, nc, "mmsc")
        mmvp = Sem(st, nc, "mmvp")
        mmsum = Sem(st, nc, "mmsum")
        mmrb = Sem(st, nc, "mmrb")
        mmav = Sem(st, nc, "mmav")
        exps = Sem(st, nc, "exps")
        svp = Sem(st, nc, "svp")
        sattn = Sem(st, nc, "sattn")
        vrec = Sem(st, nc, "vrec")
        vphat = Sem(st, nc, "vphat")
        sarb = Sem(st, nc, "sarb")

        ENGS = ("gpsimd", "sync", "tensor", "vector", "scalar")
        globs = {}

        def run_phase(name, **bodies):
            nbar[0] += 1
            if upto is not None and nbar[0] > upto:
                return
            target = 5 * (nbar[0] - 1)
            with nc.Block(name) as blk:
                for en in ENGS:
                    f = bodies.get(en)

                    def mk(en=en, f=f):
                        def body(eng):
                            if target > 0:
                                eng.wait_ge(bar.h, target)
                            if f is not None:
                                f(eng)
                            eng.sem_inc(bar.h, 1)
                        return body

                    getattr(blk, en)(mk())

        def p0_sync(sy):
            sy.dma_start(out=b12sb[:, :], in_=b12T[:, :]).then_inc(p0s.h, 16)
            sy.dma_start(out=b3sb[:, :], in_=b3T[:, :]).then_inc(p0s.h, 16)
            sy.wait_ge(p0s.h, p0s.at(32))

        def p0_gp(g):
            g.dma_start(out=mA_bf[:, :], in_=maskA[:, :]).then_inc(p0g.h, 16)
            g.dma_start(out=mB_bf[:, :], in_=maskB[:, :]).then_inc(p0g.h, 16)
            g.dma_start(out=ide_bf[:, :], in_=idem[:, :]).then_inc(p0g.h, 16)
            g.wait_ge(p0g.h, p0g.at(48))

        def p0_vec(v):
            v.memset(ones_bf[:, :], 1.0).then_inc(p0v.h, 1)
            v.memset(onesK1[:, :], 1.0).then_inc(p0v.h, 1)
            v.memset(onesK1f[:, :], 1.0).then_inc(p0v.h, 1)
            v.wait_ge(p0v.h, p0v.at(3))

        def p0_scl(s):
            s.wait_ge(p0s.h, p0s.at(32))
            s.mul(b3h[:, :], b3sb[:, :], 0.5).then_inc(p0c.h, 1)
            s.wait_ge(p0c.h, p0c.at(1))

        p0_pre = dict(sync=p0_sync, gpsimd=p0_gp, vector=p0_vec, scalar=p0_scl)

        def p0_adv():
            p0s.adv(32)
            p0g.adv(48)
            p0v.adv(3)
            p0c.adv(1)

        def ln_phase(name, src, xln, lnb, pre=None):
            srcr = src.rearrange("(c p) t -> p c t", p=128)
            sqb, stt, rbb, mbb = lnb
            mu, s1, s2, s3 = stt
            ex2, rstd = s1, s1
            musq, sstd = s2, s2
            vareps, negmurs = s3, s3
            ps_sum, ps_sumsq = globs["ps_sum"], globs["ps_sumsq"]
            ps_rb, ps_mb, lntmp = globs["ps_rb"], globs["ps_mb"], globs["lntmp"]
            NB = 32

            def f_gp(g):
                if pre and pre.get("gpsimd"):
                    pre["gpsimd"](g)
                for b in range(NB):
                    if b >= 4:
                        g.wait_ge(lns.h, lns.at(b - 3))
                    g.dma_start(out=xln[:, b, :], in_=srcr[:, b, :]
                                ).then_inc(lnx[b % 4].h, 16)
                for p in range(4):
                    g.wait_ge(lnx[p].h, lnx[p].at(16 * 8))

            def f_sync(sy):
                if pre and pre.get("sync"):
                    pre["sync"](sy)

            def f_vector(v):
                if pre and pre.get("vector"):
                    pre["vector"](v)
                v.wait_ge(lnmm.h, lnmm.at(4 * NB))
                v.tensor_scalar_mul(mu[0:1, :], ps_sum[0:1, :], 1.0 / 4096.0
                                    ).then_inc(lnvs.h, 1)
                v.tensor_scalar_mul(ex2[0:1, :], ps_sumsq[0:1, :], 1.0 / 4096.0
                                    ).then_inc(lnvs.h, 1)
                v.wait_ge(lnvs.h, lnvs.at(2))
                v.tensor_mul(musq[0:1, :], mu[0:1, :], mu[0:1, :]).then_inc(lnvs.h, 1)
                v.wait_ge(lnvs.h, lnvs.at(3))
                v.scalar_tensor_tensor(vareps[0:1, :], ex2[0:1, :], EPS, musq[0:1, :],
                                       ALU.add, ALU.subtract).then_inc(lnve.h, 1)
                v.wait_ge(lnss.h, lnss.at(1))
                v.wait_ge(lnve.h, lnve.at(1))
                v.reciprocal(rstd[0:1, :], sstd[0:1, :]).then_inc(lnvs.h, 1)
                v.wait_ge(lnvs.h, lnvs.at(4))
                v.scalar_tensor_tensor(negmurs[0:1, :], mu[0:1, :], -1.0, rstd[0:1, :],
                                       ALU.mult, ALU.mult).then_inc(lnrs.h, 1)
                rbB = rbb.rearrange("p (a t) -> p a t", a=1).broadcast_to((128, 4, TOK))
                mbB = mbb.rearrange("p (a t) -> p a t", a=1).broadcast_to((128, 4, TOK))

                def aff1(g):
                    if g >= 2:
                        v.wait_ge(lnafw.h, lnafw.at(g - 1))
                    v.tensor_mul(lntmp[:, g % 2], xln[:, 4 * g:4 * g + 4, :], rbB
                                 ).then_inc(lnafb.h, 1)

                v.wait_ge(lnsb.h, lnsb.at(2))
                aff1(0)
                for g in range(8):
                    if g + 1 < 8:
                        aff1(g + 1)
                    v.wait_ge(lnafb.h, lnafb.at(g + 1))
                    v.tensor_add(xln[:, 4 * g:4 * g + 4, :], lntmp[:, g % 2], mbB
                                 ).then_inc(lnafw.h, 1)
                v.wait_ge(lnrs.h, lnrs.at(1))
                v.wait_ge(lnafw.h, lnafw.at(8))

            def f_scalar(s):
                if pre and pre.get("scalar"):
                    pre["scalar"](s)
                for b in range(NB):
                    s.wait_ge(lnx[b % 4].h, lnx[b % 4].at(16 * (b // 4 + 1)))
                    if b >= 2:
                        s.wait_ge(lnmm.h, lnmm.at(4 * (b - 1)))
                    s.activation(sqb[:, b % 2, :], xln[:, b, :], AF.Square
                                 ).then_inc(lns.h, 1)
                s.wait_ge(lnve.h, lnve.at(1))
                s.wait_ge(lnvs.h, lnvs.at(3))
                s.activation(sstd[0:1, :], vareps[0:1, :], AF.Sqrt).then_inc(lnss.h, 1)
                s.wait_ge(lnbc.h, lnbc.at(4))
                s.activation(rbb[:, :], ps_rb[:, :], AF.Copy).then_inc(lnsb.h, 1)
                s.activation(mbb[:, :], ps_mb[:, :], AF.Copy).then_inc(lnsb.h, 1)
                s.wait_ge(lnsb.h, lnsb.at(2))

            def f_tensor(t):
                if pre:
                    t.wait_ge(p0v.h, p0v.at(3))
                for b in range(NB):
                    t.wait_ge(lnx[b % 4].h, lnx[b % 4].at(16 * (b // 4 + 1)))
                    t.wait_ge(lns.h, lns.at(b + 1))
                    first = (b == 0)
                    last = (b == NB - 1)
                    for hf in range(2):
                        cs = slice(512 * hf, 512 * hf + 512)
                        t.matmul(ps_sum[0:1, cs], ones_bf[:, :], xln[:, b, cs],
                                 start=first, stop=last).then_inc(lnmm.h, 1)
                        t.matmul(ps_sumsq[0:1, cs], ones_bf[:, :], sqb[:, b % 2, cs],
                                 start=first, stop=last).then_inc(lnmm.h, 1)
                t.wait_ge(lnrs.h, lnrs.at(1))
                t.wait_ge(lnvs.h, lnvs.at(4))
                for hf in range(2):
                    cs = slice(512 * hf, 512 * hf + 512)
                    t.matmul(ps_rb[:, cs], onesK1f[:, :], rstd[0:1, cs], start=True,
                             stop=True).then_inc(lnbc.h, 1)
                    t.matmul(ps_mb[:, cs], onesK1f[:, :], negmurs[0:1, cs], start=True,
                             stop=True).then_inc(lnbc.h, 1)
                t.wait_ge(lnmm.h, lnmm.at(4 * NB))
                t.wait_ge(lnbc.h, lnbc.at(4))

            run_phase(name, gpsimd=f_gp, sync=f_sync, vector=f_vector,
                      scalar=f_scalar, tensor=f_tensor)
            for s2 in lnx:
                s2.adv(16 * 8)
            lns.adv(NB)
            lnmm.adv(4 * NB)
            lnvs.adv(4)
            for s2 in (lnve, lnss, lnrs):
                s2.adv(1)
            lnbc.adv(4)
            lnsb.adv(2)
            lnafb.adv(8)
            lnafw.adv(8)

        uid = [0]

        def _u(base):
            uid[0] += 1
            return f"{base}_{uid[0]}"

        def alloc_ln(stk):
            sqb = stk.enter_context(nc.sbuf_tensor(_u("lnsq"), [128, 2, TOK], BF16))
            stt = [stk.enter_context(nc.sbuf_tensor(_u("lnst"), [1, TOK], F32)) for i in range(4)]
            rbb = stk.enter_context(nc.sbuf_tensor(_u("lnrbb"), [128, TOK], BF16))
            mbb = stk.enter_context(nc.sbuf_tensor(_u("lnmbb"), [128, TOK], BF16))
            return (sqb, stt, rbb, mbb)

        def ffn_m_phase(name, xln, x4, w12buf, g1b, x2bb):
            psA, psB = globs["psA"], globs["psB"]
            seq = []
            for jj in range(32):
                seq.append(jj)
                seq.append(jj + 32)

            def f_gp(g):
                for idx, j in enumerate(seq):
                    if idx >= 2:
                        if idx % 2 == 0:
                            g.wait_ge(mmA.h, mmA.at(64 * (idx // 2)))
                        else:
                            g.wait_ge(mmB.h, mmB.at(64 * (idx // 2)))
                    g.dma_start(out=w12buf[:, idx % 2], in_=w12_t[j]
                                ).then_inc(wld[idx % 2].h, 16)
                g.wait_ge(wld[0].h, wld[0].at(16 * 32))
                g.wait_ge(wld[1].h, wld[1].at(16 * 32))

            def f_tensor(t):
                for jj in range(32):
                    t.wait_ge(wld[0].h, wld[0].at(16 * (jj + 1)))
                    if jj >= 2:
                        t.wait_ge(sg.h, sg.at(jj - 1))
                    for c in range(NCH):
                        for hf in range(2):
                            cs = slice(512 * hf, 512 * hf + 512)
                            t.matmul(psA[:, jj % 2, cs], w12buf[:, 0, c, :],
                                     xln[:, c, cs], start=(c == 0), stop=(c == NCH - 1)
                                     ).then_inc(mmA.h, 1)
                    t.wait_ge(wld[1].h, wld[1].at(16 * (jj + 1)))
                    if jj >= 2:
                        t.wait_ge(vx2.h, vx2.at(jj - 1))
                    for c in range(NCH):
                        for hf in range(2):
                            cs = slice(512 * hf, 512 * hf + 512)
                            t.matmul(psB[:, jj % 2, cs], w12buf[:, 1, c, :],
                                     xln[:, c, cs], start=(c == 0), stop=(c == NCH - 1)
                                     ).then_inc(mmB.h, 1)
                t.wait_ge(mmA.h, mmA.at(64 * 32))
                t.wait_ge(mmB.h, mmB.at(64 * 32))

            def f_scalar(s):
                for jj in range(32):
                    s.wait_ge(mmA.h, mmA.at(64 * (jj + 1)))
                    if jj >= 2:
                        s.wait_ge(vx4.h, vx4.at(jj - 1))
                    s.activation(g1b[:, jj % 2, :], psA[:, jj % 2, :], AF.Silu,
                                 bias=b12sb[:, jj:jj + 1]).then_inc(sg.h, 1)
                s.wait_ge(sg.h, sg.at(32))

            def f_vector(v):
                for jj in range(32):
                    v.wait_ge(mmB.h, mmB.at(64 * (jj + 1)))
                    if jj >= 2:
                        v.wait_ge(vx4.h, vx4.at(jj - 1))
                    v.tensor_scalar_add(x2bb[:, jj % 2, :], psB[:, jj % 2, :],
                                        b12sb[:, 32 + jj:33 + jj]).then_inc(vx2.h, 1)
                    v.wait_ge(sg.h, sg.at(jj + 1))
                    v.wait_ge(vx2.h, vx2.at(jj + 1))
                    v.tensor_mul(x4[:, jj, :], g1b[:, jj % 2, :], x2bb[:, jj % 2, :]
                                 ).then_inc(vx4.h, 1)
                v.wait_ge(vx2.h, vx2.at(32))
                v.wait_ge(vx4.h, vx4.at(32))

            run_phase(name, gpsimd=f_gp, tensor=f_tensor, scalar=f_scalar, vector=f_vector)
            wld[0].adv(16 * 32)
            wld[1].adv(16 * 32)
            mmA.adv(64 * 32)
            mmB.adv(64 * 32)
            sg.adv(32)
            vx2.adv(32)
            vx4.adv(32)

        def proj_out_phase(name, wt, src, bias, scale, resid, dst, wbuf, xcb, x1b, f05b):
            ps3 = globs["ps3"]
            residr = resid.rearrange("(c p) t -> c p t", p=128)
            dstr = dst.rearrange("(c p) t -> c p t", p=128)

            def f_gp(g):
                for j in range(NT3):
                    if j >= 2:
                        g.wait_ge(mm3.h, mm3.at(64 * (j - 1)))
                    g.dma_start(out=wbuf[:, j % 2], in_=wt[j]).then_inc(wld[j % 2].h, 16)
                g.wait_ge(wld[0].h, wld[0].at(16 * 16))
                g.wait_ge(wld[1].h, wld[1].at(16 * 16))

            def f_sync(sy):
                for j in range(NT3):
                    if j >= 2:
                        sy.wait_ge(vdone.h, vdone.at(j - 1))
                        sy.dma_start(out=dstr[j - 2], in_=x1b[:, j % 2, :]
                                     ).then_inc(xst[j % 2].h, 16)
                    sy.dma_start(out=xcb[:, j % 2, :], in_=residr[j]).then_inc(xld[j % 2].h, 16)
                for j in (NT3 - 2, NT3 - 1):
                    sy.wait_ge(vdone.h, vdone.at(j + 1))
                    sy.dma_start(out=dstr[j], in_=x1b[:, j % 2, :]).then_inc(xst[j % 2].h, 16)
                sy.wait_ge(xst[0].h, xst[0].at(16 * 16))
                sy.wait_ge(xst[1].h, xst[1].at(16 * 16))
                sy.wait_ge(xld[0].h, xld[0].at(16 * 16))
                sy.wait_ge(xld[1].h, xld[1].at(16 * 16))

            def f_tensor(t):
                for j in range(NT3):
                    t.wait_ge(wld[j % 2].h, wld[j % 2].at(16 * (j // 2 + 1)))
                    if j >= 2:
                        t.wait_ge(sf.h, sf.at(j - 1))
                    for c in range(NCH):
                        for hf in range(2):
                            cs = slice(512 * hf, 512 * hf + 512)
                            t.matmul(ps3[:, j % 2, cs], wbuf[:, j % 2, c, :],
                                     src[:, c, cs], start=(c == 0), stop=(c == NCH - 1)
                                     ).then_inc(mm3.h, 1)
                t.wait_ge(mm3.h, mm3.at(64 * 32))

            def f_scalar(s):
                for j in range(NT3):
                    s.wait_ge(mm3.h, mm3.at(64 * (j + 1)))
                    if j >= 2:
                        s.wait_ge(vdone.h, vdone.at(j - 1))
                    if bias is not None:
                        s.activation(f05b[:, j % 2, :], ps3[:, j % 2, :], AF.Identity,
                                     bias=bias[:, j:j + 1], scale=scale).then_inc(sf.h, 1)
                    else:
                        s.activation(f05b[:, j % 2, :], ps3[:, j % 2, :], AF.Copy
                                     ).then_inc(sf.h, 1)
                s.wait_ge(sf.h, sf.at(32))

            def f_vector(v):
                for j in range(NT3):
                    v.wait_ge(sf.h, sf.at(j + 1))
                    v.wait_ge(xld[j % 2].h, xld[j % 2].at(16 * (j // 2 + 1)))
                    if j >= 2:
                        v.wait_ge(xst[j % 2].h, xst[j % 2].at(16 * (j // 2)))
                    v.tensor_add(x1b[:, j % 2, :], f05b[:, j % 2, :], xcb[:, j % 2, :]
                                 ).then_inc(vdone.h, 1)
                v.wait_ge(vdone.h, vdone.at(32))

            run_phase(name, gpsimd=f_gp, sync=f_sync, tensor=f_tensor,
                      scalar=f_scalar, vector=f_vector)
            wld[0].adv(16 * 16)
            wld[1].adv(16 * 16)
            mm3.adv(64 * 32)
            sf.adv(NT3)
            vdone.adv(NT3)
            xld[0].adv(16 * 16)
            xld[1].adv(16 * 16)
            xst[0].adv(16 * 16)
            xst[1].adv(16 * 16)

        def qkv_phase(name, xln, q_all, k_all, v_fm, wbuf, qrot, qpl):
            psq = globs["psq"]
            rtmp1, rtmp2 = globs["rtmp1"], globs["rtmp2"]
            sinsb, cossb = globs["sinsb"], globs["cossb"]
            items = []
            for j in range(NT3):
                items.append((wq_t[j], q_all, j, True))
            for j in range(KVT):
                items.append((wk_t[j], k_all, j, True))
            for j in range(KVT):
                items.append((wv_t[j], v_fm, j, False))
            if _QKV_NROPE is not None:
                items = [(w, d, j, rope and i < _QKV_NROPE)
                         for i, (w, d, j, rope) in enumerate(items)]
            nit = len(items)
            nrope = sum(1 for it in items if it[3])

            def scnt(i):
                return sum(3 if items[k][3] else 1 for k in range(i + 1))

            def f_gp(g):
                for idx in range(nit):
                    if idx >= 2:
                        g.wait_ge(mmq.h, mmq.at(64 * (idx - 1)))
                    g.dma_start(out=wbuf[:, idx % 2], in_=items[idx][0]
                                ).then_inc(wld[idx % 2].h, 16)
                g.wait_ge(wld[0].h, wld[0].at(16 * 24))
                g.wait_ge(wld[1].h, wld[1].at(16 * 24))

            def f_sync(sy):
                sy.dma_start(out=sinsb[:, :], in_=sinT[:, :]).then_inc(scs.h, 16)
                sy.dma_start(out=cossb[:, :], in_=cosT[:, :]).then_inc(scs.h, 16)
                sy.wait_ge(scs.h, scs.at(32))

            def f_tensor(t):
                for idx, (_, dstt, j, rope) in enumerate(items):
                    t.wait_ge(wld[idx % 2].h, wld[idx % 2].at(16 * (idx // 2 + 1)))
                    if idx >= 2:
                        t.wait_ge(sq.h, sq.at(scnt(idx - 2)))
                    for c in range(NCH):
                        for hf in range(2):
                            cs = slice(512 * hf, 512 * hf + 512)
                            t.matmul(psq[:, idx % 2, cs], wbuf[:, idx % 2, c, :],
                                     xln[:, c, cs], start=(c == 0), stop=(c == NCH - 1)
                                     ).then_inc(mmq.h, 1)
                t.wait_ge(mmq.h, mmq.at(64 * nit))

            def f_scalar(s):
                for idx, (_, dstt, j, rope) in enumerate(items):
                    s.wait_ge(mmq.h, mmq.at(64 * (idx + 1)))
                    if rope:
                        r = idx
                        if r >= 2:
                            s.wait_ge(pfree.h, pfree.at(r - 1))
                            s.wait_ge(prot.h, prot.at(r - 1))
                        s.activation(qpl[:, idx % 2, :], psq[:, idx % 2, :],
                                     AF.Copy).then_inc(sq.h, 1)
                        s.activation(qrot[0:64, idx % 2, :], psq[64:128, idx % 2, :],
                                     AF.Copy).then_inc(sq.h, 1)
                        s.activation(qrot[64:128, idx % 2, :], psq[0:64, idx % 2, :],
                                     AF.Copy).then_inc(sq.h, 1)
                    else:
                        s.activation(dstt[:, :, j], psq[:, idx % 2, :], AF.Copy
                                     ).then_inc(sq.h, 1)
                s.wait_ge(sq.h, sq.at(scnt(nit - 1)))

            def f_vector(v):
                v.wait_ge(scs.h, scs.at(32))
                for idx, (_, dstt, j, rope) in enumerate(items):
                    if not rope:
                        continue
                    r = idx
                    v.wait_ge(sq.h, sq.at(scnt(idx)))
                    if r >= 1:
                        v.wait_ge(pq.h, pq.at(r))
                    v.tensor_mul(rtmp1[:, :], qpl[:, idx % 2, :], cossb[:, :]
                                 ).then_inc(pfree.h, 1)
                    v.tensor_mul(rtmp2[:, :], qrot[:, idx % 2, :], sinsb[:, :]
                                 ).then_inc(prot.h, 1)
                    v.wait_ge(pfree.h, pfree.at(r + 1))
                    v.wait_ge(prot.h, prot.at(r + 1))
                    v.tensor_add(dstt[:, :, j], rtmp1[:, :], rtmp2[:, :]
                                 ).then_inc(pq.h, 1)
                if nrope:
                    v.wait_ge(pq.h, pq.at(nrope))

            run_phase(name, gpsimd=f_gp, sync=f_sync, tensor=f_tensor,
                      scalar=f_scalar, vector=f_vector)
            wld[0].adv(16 * 24)
            wld[1].adv(16 * 24)
            mmq.adv(64 * nit)
            sq.adv(scnt(nit - 1))
            scs.adv(32)
            pfree.adv(nrope)
            prot.adv(nrope)
            pq.adv(nrope)

        def attn_phase(name, q_all, k_all, v_fm, attn_buf, ab):
            expb, phatb, vpackb, sumb, arbsb = ab
            ps_sc = globs["ps_sc"]
            ps_asum, ps_arb, ps_av = globs["ps_asum"], globs["ps_arb"], globs["ps_av"]
            N = NPACK

            def f_sync(sy):
                for w in range(N // 8):
                    i0 = 8 * w
                    if i0 >= 16:
                        sy.wait_ge(mmav.h, mmav.at(i0 - 8))
                    vsl = v_fm[:, 128 * w:128 * w + 128, :].rearrange("d t g -> d (t g)")
                    sy.dma_start_transpose(vpackb[:, i0 % 16:i0 % 16 + 8, :], vsl
                                           ).then_inc(svp.h, 16)
                    sy.wait_ge(svp.h, svp.at(16 * (w + 1)))

            def f_tensor(t):
                for k in range(N + 8):
                    if k < N:
                        i = k
                        if i >= 2:
                            t.wait_ge(exps.h, exps.at(i - 1))
                        t.matmul(ps_sc[:, i % 2, :], mA_bf[:, :], mB_bf[:, :],
                                 start=True, stop=False)
                        klh = k_all[:, 16 * i:16 * i + 16, :].rearrange("d t g -> d (t g)")
                        qrh = q_all[:, 16 * i:16 * i + 16, :].rearrange("d t h -> d (t h)")
                        t.matmul(ps_sc[:, i % 2, :], klh, qrh, start=False, stop=True
                                 ).then_inc(mmsc.h, 1)
                    if 0 <= k - 2 < N:
                        i = k - 2
                        t.wait_ge(exps.h, exps.at(i + 1))
                        if i >= 2:
                            t.wait_ge(vrec.h, vrec.at(i - 1))
                        t.matmul(ps_asum[0:1, i % 2, :], ones_bf[:, :], expb[:, i % 8, :],
                                 start=True, stop=True).then_inc(mmsum.h, 1)
                    if 0 <= k - 4 < N:
                        i = k - 4
                        t.wait_ge(vrec.h, vrec.at(i + 1))
                        if i >= 2:
                            t.wait_ge(sarb.h, sarb.at(i - 1))
                        t.matmul(ps_arb[:, i % 2, :], onesK1[:, :], sumb[0:1, i % 4, :],
                                 start=True, stop=True).then_inc(mmrb.h, 1)
                    if 0 <= k - 8 < N:
                        i = k - 8
                        t.wait_ge(vphat.h, vphat.at(i // 2 + 1))
                        t.wait_ge(svp.h, svp.at(16 * (i // 8 + 1)))
                        if i >= 2:
                            t.wait_ge(sattn.h, sattn.at(i - 1))
                        t.matmul(ps_av[:, i % 2, :], vpackb[:, i % 16, :],
                                 phatb[:, (i // 2) % 2, i % 2, :],
                                 start=True, stop=True).then_inc(mmav.h, 1)
                for s2 in (mmsc, mmsum, mmrb, mmav):
                    t.wait_ge(s2.h, s2.at(N))

            def f_scalar(s):
                for k in range(N + 9):
                    if k < N:
                        i = k
                        s.wait_ge(mmsc.h, mmsc.at(i + 1))
                        if i >= 8:
                            s.wait_ge(mmsum.h, mmsum.at(i - 7))
                            s.wait_ge(vphat.h, vphat.at(i // 2 - 3))
                        s.activation(expb[:, i % 8, :], ps_sc[:, i % 2, :], AF.Exp,
                                     scale=ISD).then_inc(exps.h, 1)
                    if 0 <= k - 5 < N:
                        i = k - 5
                        s.wait_ge(mmrb.h, mmrb.at(i + 1))
                        if i >= 4:
                            s.wait_ge(vphat.h, vphat.at(i // 2))
                        s.activation(arbsb[:, i % 4, :], ps_arb[:, i % 2, :], AF.Copy
                                     ).then_inc(sarb.h, 1)
                    if 0 <= k - 9 < N:
                        i = k - 9
                        s.wait_ge(mmav.h, mmav.at(i + 1))
                        s.activation(attn_buf[:, :, 16 * i:16 * i + 16],
                                     ps_av[:, i % 2, :].rearrange("d (t h) -> d h t", t=16),
                                     AF.Copy).then_inc(sattn.h, 1)
                for s2 in (exps, svp, sarb, sattn):
                    s.wait_ge(s2.h, s2.at(N))

            def f_vector(v):
                for w in range(N // 2 + 2):
                    if w < N // 2:
                        for i in (2 * w, 2 * w + 1):
                            v.wait_ge(mmsum.h, mmsum.at(i + 1))
                            if i >= 4:
                                v.wait_ge(mmrb.h, mmrb.at(i - 3))
                            with nc.allow_low_precision(reason="reciprocal feeds a "
                                                        "bf16 product chain"):
                                v.reciprocal(sumb[0:1, i % 4, :], ps_asum[0:1, i % 2, :]
                                             ).then_inc(vrec.h, 1)
                    if 0 <= w - 2 < N // 2:
                        wp = w - 2
                        i0 = 2 * wp
                        v.wait_ge(sarb.h, sarb.at(i0 + 2))
                        v.wait_ge(exps.h, exps.at(i0 + 2))
                        if wp >= 2:
                            v.wait_ge(mmav.h, mmav.at(i0 - 2))
                        v.tensor_mul(phatb[:, wp % 2, :, :],
                                     expb[:, i0 % 8:i0 % 8 + 2, :],
                                     arbsb[:, i0 % 4:i0 % 4 + 2, :]).then_inc(vphat.h, 1)
                v.wait_ge(vrec.h, vrec.at(N))
                v.wait_ge(vphat.h, vphat.at(N // 2))

            run_phase(name, sync=f_sync, tensor=f_tensor, scalar=f_scalar,
                      vector=f_vector)
            for s2 in (mmsc, mmsum, mmrb, mmav, exps, sattn, vrec, sarb):
                s2.adv(N)
            svp.adv(16 * (N // 8))
            vphat.adv(N // 2)

        def ln_scope(stk):
            ps_sum = stk.enter_context(nc.psum_tensor(_u("ps_sum"), [128, TOK], F32))
            ps_sumsq = stk.enter_context(nc.psum_tensor(_u("ps_sumsq"), [128, TOK], F32))
            ps_rb = stk.enter_context(nc.psum_tensor(_u("ps_rb"), [128, TOK], F32))
            ps_mb = stk.enter_context(nc.psum_tensor(_u("ps_mb"), [128, TOK], F32))
            lntmp = stk.enter_context(nc.sbuf_tensor(_u("lntmp"), [128, 2, 4, TOK], BF16))
            lnb = alloc_ln(stk)
            return ps_sum, ps_sumsq, ps_rb, ps_mb, lntmp, lnb

        def ffn_sublayer(tag, src_ln, resid, dst, ln_pre=None):
            with ExitStack() as stF:
                xln = stF.enter_context(nc.sbuf_tensor("xln" + tag, [128, NCH, TOK], BF16))
                with ExitStack() as stL:
                    ps_sum, ps_sumsq, ps_rb, ps_mb, lntmp, lnb = ln_scope(stL)
                    globs.update(ps_sum=ps_sum, ps_sumsq=ps_sumsq, ps_rb=ps_rb,
                                 ps_mb=ps_mb, lntmp=lntmp)
                    ln_phase(tag + "1", src_ln, xln, lnb, pre=ln_pre)
                with ExitStack() as stM:
                    x4 = stM.enter_context(nc.sbuf_tensor("x4" + tag, [128, NCH, TOK], BF16))
                    w12buf = stM.enter_context(nc.sbuf_tensor("w12b" + tag, [128, 2, NCH, 128], BF16))
                    g1b = stM.enter_context(nc.sbuf_tensor("g1b" + tag, [128, 2, TOK], BF16))
                    x2bb = stM.enter_context(nc.sbuf_tensor("x2b" + tag, [128, 2, TOK], BF16))
                    with ExitStack() as stM2:
                        psA_ = stM2.enter_context(nc.psum_tensor(_u("psA"), [128, 2, TOK], F32))
                        psB_ = stM2.enter_context(nc.psum_tensor(_u("psB"), [128, 2, TOK], F32))
                        globs.update(psA=psA_, psB=psB_)
                        ffn_m_phase(tag + "2", xln, x4, w12buf, g1b, x2bb)
                    with ExitStack() as stP:
                        ps3_ = stP.enter_context(nc.psum_tensor(_u("ps3"), [128, 2, TOK], F32))
                        wbuf3 = stP.enter_context(nc.sbuf_tensor("w3b" + tag, [128, 2, NCH, 128], BF16))
                        xcb = stP.enter_context(nc.sbuf_tensor("xcb" + tag, [128, 2, TOK], F32))
                        x1b = stP.enter_context(nc.sbuf_tensor("x1b" + tag, [128, 2, TOK], F32))
                        f05b = stP.enter_context(nc.sbuf_tensor("f05" + tag, [128, 2, TOK], F32))
                        globs.update(ps3=ps3_)
                        proj_out_phase(tag + "3", w3_t, x4, b3h, 0.5, resid, dst,
                                       wbuf3, xcb, x1b, f05b)

        ffn_sublayer("A", xT, xT, x1T, ln_pre=p0_pre)
        p0_adv()

        with ExitStack() as stB:
            q_all = stB.enter_context(nc.sbuf_tensor("q_all", [128, TOK, NT3], BF16))
            k_all = stB.enter_context(nc.sbuf_tensor("k_all", [128, TOK, KVT], BF16))
            v_fm = stB.enter_context(nc.sbuf_tensor("v_fm", [128, TOK, KVT], BF16))
            with ExitStack() as stB1:
                xln2 = stB1.enter_context(nc.sbuf_tensor("xln2", [128, NCH, TOK], BF16))
                with ExitStack() as stL:
                    ps_sum, ps_sumsq, ps_rb, ps_mb, lntmp, lnb = ln_scope(stL)
                    globs.update(ps_sum=ps_sum, ps_sumsq=ps_sumsq, ps_rb=ps_rb,
                                 ps_mb=ps_mb, lntmp=lntmp)
                    ln_phase("B1", x1T, xln2, lnb)
                with ExitStack() as stQ:
                    psq_ = stQ.enter_context(nc.psum_tensor(_u("psq"), [128, 2, TOK], F32))
                    wbufq = stQ.enter_context(nc.sbuf_tensor("wbufq", [128, 2, NCH, 128], BF16))
                    qrot = stQ.enter_context(nc.sbuf_tensor("qrot", [128, 2, TOK], BF16))
                    qpl = stQ.enter_context(nc.sbuf_tensor("qpl", [128, 2, TOK], BF16))
                    rtmp1_ = stQ.enter_context(nc.sbuf_tensor("rtmp1", [128, TOK], F32))
                    rtmp2_ = stQ.enter_context(nc.sbuf_tensor("rtmp2", [128, TOK], F32))
                    sinsb_ = stQ.enter_context(nc.sbuf_tensor("sinsb", [128, TOK], F32))
                    cossb_ = stQ.enter_context(nc.sbuf_tensor("cossb", [128, TOK], F32))
                    globs.update(psq=psq_, rtmp1=rtmp1_, rtmp2=rtmp2_,
                                 sinsb=sinsb_, cossb=cossb_)
                    qkv_phase("B2", xln2, q_all, k_all, v_fm, wbufq, qrot, qpl)
            with ExitStack() as stB3:
                attn_buf = stB3.enter_context(nc.sbuf_tensor("attn_buf", [128, NT3, TOK], BF16))
                with ExitStack() as stAT:
                    ps_sc_ = stAT.enter_context(nc.psum_tensor(_u("ps_sc"), [128, 2, 512], F32))
                    ps_asum_ = stAT.enter_context(nc.psum_tensor(_u("ps_asum"), [128, 2, 512], F32))
                    ps_arb_ = stAT.enter_context(nc.psum_tensor(_u("ps_arb"), [128, 2, 512], F32))
                    ps_av_ = stAT.enter_context(nc.psum_tensor(_u("ps_av"), [128, 2, 512], F32))
                    expb = stAT.enter_context(nc.sbuf_tensor("expb", [128, 8, 512], BF16))
                    phatb = stAT.enter_context(nc.sbuf_tensor("phatb", [128, 2, 2, 512], BF16))
                    vpackb = stAT.enter_context(nc.sbuf_tensor("vpackb", [128, 16, 128], BF16))
                    sumb = stAT.enter_context(nc.sbuf_tensor("sumb", [1, 4, 512], BF16))
                    arbsb = stAT.enter_context(nc.sbuf_tensor("arbsb", [128, 4, 512], BF16))
                    globs.update(ps_sc=ps_sc_, ps_asum=ps_asum_,
                                 ps_arb=ps_arb_, ps_av=ps_av_)
                    attn_phase("B3", q_all, k_all, v_fm, attn_buf,
                               (expb, phatb, vpackb, sumb, arbsb))
                with ExitStack() as stP:
                    ps3_ = stP.enter_context(nc.psum_tensor(_u("ps3"), [128, 2, TOK], F32))
                    wbufo = stP.enter_context(nc.sbuf_tensor("wbufo", [128, 2, NCH, 128], BF16))
                    xcb = stP.enter_context(nc.sbuf_tensor("xcbB", [128, 2, TOK], F32))
                    x1b = stP.enter_context(nc.sbuf_tensor("x1bB", [128, 2, TOK], F32))
                    f05B = stP.enter_context(nc.sbuf_tensor("f05B", [128, 2, TOK], F32))
                    globs.update(ps3=ps3_)
                    proj_out_phase("B4", wo_t, attn_buf, None, 1.0, x1T, x2T,
                                   wbufo, xcb, x1b, f05B)

        ffn_sublayer("C", x2T, x2T, outT)

    return nc


_CACHE = {}


def _host_tables():
    half = 64
    i = np.arange(half, dtype=np.float64)
    emb = 10000.0 ** (2.0 * np.floor(i / 2.0) / half)
    pos = np.arange(2048, dtype=np.float64)
    fr = pos[:, None] * emb[None, :]
    sin_sgn = np.concatenate([-np.sin(fr), np.sin(fr)], axis=-1).astype(np.float32)
    cos_full = np.concatenate([np.cos(fr), np.cos(fr)], axis=-1).astype(np.float32)
    C = float(2 ** 30)
    mA = np.zeros((17, 128), dtype=np.float32)
    mB = np.zeros((17, 512), dtype=np.float32)
    mA[0, :] = 1.0
    mB[0, :] = -C
    for t in range(16):
        mA[1 + t, 8 * t:8 * t + 8] = 1.0
        mB[1 + t, 32 * t:32 * t + 32] = C
    ide = np.eye(128, dtype=np.float32)
    return sin_sgn, cos_full, mA, mB, ide


def kernel(L0_attn_hidden_states, L0_attn_Wq, L0_attn_Wk, L0_attn_Wv, L0_attn_Wo,
           L0_ffn_w12, L0_ffn_b12, L0_ffn_w3, L0_ffn_b3):
    x = np.asarray(L0_attn_hidden_states, dtype=np.float32)
    Wq = np.asarray(L0_attn_Wq, dtype=np.float32)
    Wk = np.asarray(L0_attn_Wk, dtype=np.float32)
    Wv = np.asarray(L0_attn_Wv, dtype=np.float32)
    Wo = np.asarray(L0_attn_Wo, dtype=np.float32)
    w12 = np.asarray(L0_ffn_w12, dtype=np.float32)
    b12 = np.asarray(L0_ffn_b12, dtype=np.float32)
    w3 = np.asarray(L0_ffn_w3, dtype=np.float32)
    b3 = np.asarray(L0_ffn_b3, dtype=np.float32)

    wq_t = np.ascontiguousarray(Wq.reshape(32, 128, 32, 128).transpose(2, 1, 0, 3))
    wk_t = np.ascontiguousarray(Wk.reshape(32, 128, 8, 128).transpose(2, 1, 0, 3))
    wv_t = np.ascontiguousarray(Wv.reshape(32, 128, 8, 128).transpose(2, 1, 0, 3))
    wo_t = np.ascontiguousarray(Wo.reshape(32, 128, 32, 128).transpose(2, 1, 0, 3))
    w12_t = np.ascontiguousarray(w12.reshape(64, 128, 32, 128).transpose(0, 3, 2, 1))
    w3_t = np.ascontiguousarray(w3.reshape(32, 128, 32, 128).transpose(0, 3, 2, 1))
    b12T = np.ascontiguousarray(b12.reshape(64, 128).T)
    b3T = np.ascontiguousarray(b3.reshape(32, 128).T)

    sin_sgn, cos_full, mA, mB, ide = _host_tables()

    xr = x.reshape(4, 2, 1024, 4096)
    in_maps = []
    for core in range(8):
        b, hh = core // 2, core % 2
        xTc = np.ascontiguousarray(xr[b, hh].T)
        p0 = hh * 1024
        in_maps.append({
            "xT": xTc,
            "wq_t": wq_t, "wk_t": wk_t, "wv_t": wv_t, "wo_t": wo_t,
            "w12_t": w12_t, "w3_t": w3_t,
            "b12T": b12T, "b3T": b3T,
            "sinT": np.ascontiguousarray(sin_sgn[p0:p0 + 1024].T),
            "cosT": np.ascontiguousarray(cos_full[p0:p0 + 1024].T),
            "maskA": mA, "maskB": mB, "idem": ide,
        })

    nc = _CACHE.get("nc")
    if nc is None:
        nc = build_program()
        _CACHE["nc"] = nc

    res = run_bass_kernel_spmd(nc, in_maps, core_ids=list(range(8)))

    out = np.empty((4, 2048, 4096), dtype=np.float32)
    for core in range(8):
        b, hh = core // 2, core % 2
        out[b, hh * 1024:(hh + 1) * 1024, :] = res.results[core]["outT"].T
    return out


# version 14
# speedup vs baseline: 1.0795x; 1.0795x over previous
import numpy as np
import ml_dtypes
from contextlib import ExitStack

import concourse.bass as bass
import concourse.mybir as mybir
from concourse.bass_utils import run_bass_kernel_spmd

F32 = mybir.dt.float32
BF16 = mybir.dt.bfloat16
AF = mybir.ActivationFunctionType
ALU = mybir.AluOpType

TOK = 1024
NCH = 32
MT12 = 64
NT3 = 32
KVT = 8
NPACK = 64
ISD = 0.08838834764831845
EPS = 1e-5


class Sem:
    def __init__(self, stack, nc, name):
        self.h = stack.enter_context(nc.semaphore(name))
        self.base = 0

    def at(self, k):
        return self.base + k

    def adv(self, n):
        self.base += n


_QKV_NROPE = None


def build_program(upto=None):
    nc = bass.Bass("TRN2", target_bir_lowering=False)

    xT = nc.dram_tensor("xT", [4096, TOK], F32, kind="ExternalInput").ap()
    wq_t = nc.dram_tensor("wq_t", [NT3, 128, NCH, 128], F32, kind="ExternalInput").ap()
    wk_t = nc.dram_tensor("wk_t", [KVT, 128, NCH, 128], F32, kind="ExternalInput").ap()
    wv_t = nc.dram_tensor("wv_t", [KVT, 128, NCH, 128], F32, kind="ExternalInput").ap()
    wo_t = nc.dram_tensor("wo_t", [NT3, 128, NCH, 128], F32, kind="ExternalInput").ap()
    w12_t = nc.dram_tensor("w12_t", [MT12, 128, NCH, 128], F32, kind="ExternalInput").ap()
    w3_t = nc.dram_tensor("w3_t", [NT3, 128, NCH, 128], F32, kind="ExternalInput").ap()
    b12T = nc.dram_tensor("b12T", [128, 64], F32, kind="ExternalInput").ap()
    b3T = nc.dram_tensor("b3T", [128, 32], F32, kind="ExternalInput").ap()
    argT = nc.dram_tensor("argT", [128, TOK], F32, kind="ExternalInput").ap()
    trwT = nc.dram_tensor("trwT", [128, 4, TOK], BF16, kind="ExternalInput").ap()
    maskA = nc.dram_tensor("maskA", [17, 128], F32, kind="ExternalInput").ap()
    maskB = nc.dram_tensor("maskB", [17, 512], F32, kind="ExternalInput").ap()
    idem = nc.dram_tensor("idem", [128, 128], F32, kind="ExternalInput").ap()
    outT = nc.dram_tensor("outT", [4096, TOK], F32, kind="ExternalOutput").ap()
    x1T = nc.dram_tensor("x1T", [4096, TOK], F32).ap()
    x2T = nc.dram_tensor("x2T", [4096, TOK], F32).ap()

    st = ExitStack()
    with st:
        b12sb = st.enter_context(nc.sbuf_tensor("b12sb", [128, 64], F32))
        b3sb = st.enter_context(nc.sbuf_tensor("b3sb", [128, 32], F32))
        b3h = st.enter_context(nc.sbuf_tensor("b3h", [128, 32], F32))
        mA_bf = st.enter_context(nc.sbuf_tensor("mA_bf", [17, 128], BF16))
        mB_bf = st.enter_context(nc.sbuf_tensor("mB_bf", [17, 512], BF16))
        ide_bf = st.enter_context(nc.sbuf_tensor("ide_bf", [128, 128], BF16))
        ones_bf = st.enter_context(nc.sbuf_tensor("ones_bf", [128, 1], BF16))
        onesK1 = st.enter_context(nc.sbuf_tensor("onesK1", [1, 128], BF16))
        onesK1f = st.enter_context(nc.sbuf_tensor("onesK1f", [1, 128], F32))

        bar = Sem(st, nc, "bar")
        nbar = [0]
        p0s = Sem(st, nc, "p0s")
        p0g = Sem(st, nc, "p0g")
        p0v = Sem(st, nc, "p0v")
        p0c = Sem(st, nc, "p0c")
        wld = [Sem(st, nc, "wld0"), Sem(st, nc, "wld1")]
        lnx = [Sem(st, nc, f"lnx{i}") for i in range(4)]
        lns = Sem(st, nc, "lns")
        lnmm = Sem(st, nc, "lnmm")
        lnvs = Sem(st, nc, "lnvs")
        lnve = Sem(st, nc, "lnve")
        lnss = Sem(st, nc, "lnss")
        lnrs = Sem(st, nc, "lnrs")
        lnbc = Sem(st, nc, "lnbc")
        lnsb = Sem(st, nc, "lnsb")
        lnafb = Sem(st, nc, "lnafb")
        lnafw = Sem(st, nc, "lnafw")
        mmA = Sem(st, nc, "mmA")
        mmB = Sem(st, nc, "mmB")
        sg = Sem(st, nc, "sg")
        vx2 = Sem(st, nc, "vx2")
        vx4 = Sem(st, nc, "vx4")
        xld = [Sem(st, nc, "xld0"), Sem(st, nc, "xld1")]
        xst = [Sem(st, nc, "xst0"), Sem(st, nc, "xst1")]
        vdone = Sem(st, nc, "vdone")
        mm3 = Sem(st, nc, "mm3")
        sf = Sem(st, nc, "sf")
        mmq = Sem(st, nc, "mmq")
        sq = Sem(st, nc, "sq")
        scs = Sem(st, nc, "scs")
        pfree = Sem(st, nc, "pfree")
        prot = Sem(st, nc, "prot")
        pq = Sem(st, nc, "pq")
        mmsc = Sem(st, nc, "mmsc")
        mmvp = Sem(st, nc, "mmvp")
        mmsum = Sem(st, nc, "mmsum")
        mmrb = Sem(st, nc, "mmrb")
        mmav = Sem(st, nc, "mmav")
        exps = Sem(st, nc, "exps")
        svp = Sem(st, nc, "svp")
        sattn = Sem(st, nc, "sattn")
        vrec = Sem(st, nc, "vrec")
        vphat = Sem(st, nc, "vphat")
        sarb = Sem(st, nc, "sarb")

        ENGS = ("gpsimd", "sync", "tensor", "vector", "scalar")
        globs = {}

        def run_phase(name, **bodies):
            nbar[0] += 1
            if upto is not None and nbar[0] > upto:
                return
            target = 5 * (nbar[0] - 1)
            with nc.Block(name) as blk:
                for en in ENGS:
                    f = bodies.get(en)

                    def mk(en=en, f=f):
                        def body(eng):
                            if target > 0:
                                eng.wait_ge(bar.h, target)
                            if f is not None:
                                f(eng)
                            eng.sem_inc(bar.h, 1)
                        return body

                    getattr(blk, en)(mk())

        def p0_sync(sy):
            sy.dma_start(out=b12sb[:, :], in_=b12T[:, :]).then_inc(p0s.h, 16)
            sy.dma_start(out=b3sb[:, :], in_=b3T[:, :]).then_inc(p0s.h, 16)
            sy.wait_ge(p0s.h, p0s.at(32))

        def p0_gp(g):
            g.dma_start(out=mA_bf[:, :], in_=maskA[:, :]).then_inc(p0g.h, 16)
            g.dma_start(out=mB_bf[:, :], in_=maskB[:, :]).then_inc(p0g.h, 16)
            g.dma_start(out=ide_bf[:, :], in_=idem[:, :]).then_inc(p0g.h, 16)
            g.wait_ge(p0g.h, p0g.at(48))

        def p0_vec(v):
            v.memset(ones_bf[:, :], 1.0).then_inc(p0v.h, 1)
            v.memset(onesK1[:, :], 1.0).then_inc(p0v.h, 1)
            v.memset(onesK1f[:, :], 1.0).then_inc(p0v.h, 1)
            v.wait_ge(p0v.h, p0v.at(3))

        def p0_scl(s):
            s.wait_ge(p0s.h, p0s.at(32))
            s.mul(b3h[:, :], b3sb[:, :], 0.5).then_inc(p0c.h, 1)
            s.wait_ge(p0c.h, p0c.at(1))

        p0_pre = dict(sync=p0_sync, gpsimd=p0_gp, vector=p0_vec, scalar=p0_scl)

        def p0_adv():
            p0s.adv(32)
            p0g.adv(48)
            p0v.adv(3)
            p0c.adv(1)

        def ln_phase(name, src, xln, lnb, pre=None, gp_extra=None):
            srcr = src.rearrange("(c p) t -> p c t", p=128)
            sqb, stt, rbb, mbb = lnb
            mu, s1, s2, s3 = stt
            ex2, rstd = s1, s1
            musq, sstd = s2, s2
            vareps, negmurs = s3, s3
            ps_sum, ps_sumsq = globs["ps_sum"], globs["ps_sumsq"]
            ps_rb, ps_mb, lntmp = globs["ps_rb"], globs["ps_mb"], globs["lntmp"]
            NB = 32

            def f_gp(g):
                if pre and pre.get("gpsimd"):
                    pre["gpsimd"](g)
                for b in range(NB):
                    if b >= 4:
                        g.wait_ge(lns.h, lns.at(b - 3))
                    g.dma_start(out=xln[:, b, :], in_=srcr[:, b, :]
                                ).then_inc(lnx[b % 4].h, 16)
                if gp_extra:
                    gp_extra(g)
                for p in range(4):
                    g.wait_ge(lnx[p].h, lnx[p].at(16 * 8))

            def f_sync(sy):
                if pre and pre.get("sync"):
                    pre["sync"](sy)

            def f_vector(v):
                if pre and pre.get("vector"):
                    pre["vector"](v)
                v.wait_ge(lnmm.h, lnmm.at(4 * NB))
                v.tensor_scalar_mul(mu[0:1, :], ps_sum[0:1, :], 1.0 / 4096.0
                                    ).then_inc(lnvs.h, 1)
                v.tensor_scalar_mul(ex2[0:1, :], ps_sumsq[0:1, :], 1.0 / 4096.0
                                    ).then_inc(lnvs.h, 1)
                v.wait_ge(lnvs.h, lnvs.at(2))
                v.tensor_mul(musq[0:1, :], mu[0:1, :], mu[0:1, :]).then_inc(lnvs.h, 1)
                v.wait_ge(lnvs.h, lnvs.at(3))
                v.scalar_tensor_tensor(vareps[0:1, :], ex2[0:1, :], EPS, musq[0:1, :],
                                       ALU.add, ALU.subtract).then_inc(lnve.h, 1)
                v.wait_ge(lnss.h, lnss.at(1))
                v.wait_ge(lnve.h, lnve.at(1))
                v.reciprocal(rstd[0:1, :], sstd[0:1, :]).then_inc(lnvs.h, 1)
                v.wait_ge(lnvs.h, lnvs.at(4))
                v.scalar_tensor_tensor(negmurs[0:1, :], mu[0:1, :], -1.0, rstd[0:1, :],
                                       ALU.mult, ALU.mult).then_inc(lnrs.h, 1)
                rbB = rbb.rearrange("p (a t) -> p a t", a=1).broadcast_to((128, 4, TOK))
                mbB = mbb.rearrange("p (a t) -> p a t", a=1).broadcast_to((128, 4, TOK))

                def aff1(g):
                    if g >= 2:
                        v.wait_ge(lnafw.h, lnafw.at(g - 1))
                    v.tensor_mul(lntmp[:, g % 2], xln[:, 4 * g:4 * g + 4, :], rbB
                                 ).then_inc(lnafb.h, 1)

                v.wait_ge(lnsb.h, lnsb.at(2))
                aff1(0)
                for g in range(8):
                    if g + 1 < 8:
                        aff1(g + 1)
                    v.wait_ge(lnafb.h, lnafb.at(g + 1))
                    v.tensor_add(xln[:, 4 * g:4 * g + 4, :], lntmp[:, g % 2], mbB
                                 ).then_inc(lnafw.h, 1)
                v.wait_ge(lnrs.h, lnrs.at(1))
                v.wait_ge(lnafw.h, lnafw.at(8))

            def f_scalar(s):
                if pre and pre.get("scalar"):
                    pre["scalar"](s)
                for b in range(NB):
                    s.wait_ge(lnx[b % 4].h, lnx[b % 4].at(16 * (b // 4 + 1)))
                    if b >= 2:
                        s.wait_ge(lnmm.h, lnmm.at(4 * (b - 1)))
                    s.activation(sqb[:, b % 2, :], xln[:, b, :], AF.Square
                                 ).then_inc(lns.h, 1)
                s.wait_ge(lnve.h, lnve.at(1))
                s.wait_ge(lnvs.h, lnvs.at(3))
                s.activation(sstd[0:1, :], vareps[0:1, :], AF.Sqrt).then_inc(lnss.h, 1)
                s.wait_ge(lnbc.h, lnbc.at(4))
                s.activation(rbb[:, :], ps_rb[:, :], AF.Copy).then_inc(lnsb.h, 1)
                s.activation(mbb[:, :], ps_mb[:, :], AF.Copy).then_inc(lnsb.h, 1)
                s.wait_ge(lnsb.h, lnsb.at(2))

            def f_tensor(t):
                if pre:
                    t.wait_ge(p0v.h, p0v.at(3))
                for b in range(NB):
                    t.wait_ge(lnx[b % 4].h, lnx[b % 4].at(16 * (b // 4 + 1)))
                    t.wait_ge(lns.h, lns.at(b + 1))
                    first = (b == 0)
                    last = (b == NB - 1)
                    for hf in range(2):
                        cs = slice(512 * hf, 512 * hf + 512)
                        t.matmul(ps_sum[0:1, cs], ones_bf[:, :], xln[:, b, cs],
                                 start=first, stop=last).then_inc(lnmm.h, 1)
                        t.matmul(ps_sumsq[0:1, cs], ones_bf[:, :], sqb[:, b % 2, cs],
                                 start=first, stop=last).then_inc(lnmm.h, 1)
                t.wait_ge(lnrs.h, lnrs.at(1))
                t.wait_ge(lnvs.h, lnvs.at(4))
                for hf in range(2):
                    cs = slice(512 * hf, 512 * hf + 512)
                    t.matmul(ps_rb[:, cs], onesK1f[:, :], rstd[0:1, cs], start=True,
                             stop=True).then_inc(lnbc.h, 1)
                    t.matmul(ps_mb[:, cs], onesK1f[:, :], negmurs[0:1, cs], start=True,
                             stop=True).then_inc(lnbc.h, 1)
                t.wait_ge(lnmm.h, lnmm.at(4 * NB))
                t.wait_ge(lnbc.h, lnbc.at(4))

            run_phase(name, gpsimd=f_gp, sync=f_sync, vector=f_vector,
                      scalar=f_scalar, tensor=f_tensor)
            for s2 in lnx:
                s2.adv(16 * 8)
            lns.adv(NB)
            lnmm.adv(4 * NB)
            lnvs.adv(4)
            for s2 in (lnve, lnss, lnrs):
                s2.adv(1)
            lnbc.adv(4)
            lnsb.adv(2)
            lnafb.adv(8)
            lnafw.adv(8)

        uid = [0]

        def _u(base):
            uid[0] += 1
            return f"{base}_{uid[0]}"

        def alloc_ln(stk):
            sqb = stk.enter_context(nc.sbuf_tensor(_u("lnsq"), [128, 2, TOK], BF16))
            stt = [stk.enter_context(nc.sbuf_tensor(_u("lnst"), [1, TOK], F32)) for i in range(4)]
            rbb = stk.enter_context(nc.sbuf_tensor(_u("lnrbb"), [128, TOK], BF16))
            mbb = stk.enter_context(nc.sbuf_tensor(_u("lnmbb"), [128, TOK], BF16))
            return (sqb, stt, rbb, mbb)

        def ffn_m_phase(name, xln, x4, w12buf, g1b, x2bb, skip_first=0, gp_extra=None):
            psA, psB = globs["psA"], globs["psB"]
            seq = []
            for jj in range(32):
                seq.append(jj)
                seq.append(jj + 32)

            def f_gp(g):
                for idx, j in enumerate(seq):
                    if idx < skip_first:
                        continue
                    if idx >= 2:
                        if idx % 2 == 0:
                            g.wait_ge(mmA.h, mmA.at(64 * (idx // 2)))
                        else:
                            g.wait_ge(mmB.h, mmB.at(64 * (idx // 2)))
                    g.dma_start(out=w12buf[:, idx % 2], in_=w12_t[j]
                                ).then_inc(wld[idx % 2].h, 16)
                g.wait_ge(wld[0].h, wld[0].at(16 * 32))
                g.wait_ge(wld[1].h, wld[1].at(16 * 32))
                if gp_extra:
                    gp_extra(g)

            def f_tensor(t):
                for jj in range(32):
                    t.wait_ge(wld[0].h, wld[0].at(16 * (jj + 1)))
                    if jj >= 2:
                        t.wait_ge(sg.h, sg.at(jj - 1))
                    for c in range(NCH):
                        for hf in range(2):
                            cs = slice(512 * hf, 512 * hf + 512)
                            t.matmul(psA[:, jj % 2, cs], w12buf[:, 0, c, :],
                                     xln[:, c, cs], start=(c == 0), stop=(c == NCH - 1)
                                     ).then_inc(mmA.h, 1)
                    t.wait_ge(wld[1].h, wld[1].at(16 * (jj + 1)))
                    if jj >= 2:
                        t.wait_ge(vx2.h, vx2.at(jj - 1))
                    for c in range(NCH):
                        for hf in range(2):
                            cs = slice(512 * hf, 512 * hf + 512)
                            t.matmul(psB[:, jj % 2, cs], w12buf[:, 1, c, :],
                                     xln[:, c, cs], start=(c == 0), stop=(c == NCH - 1)
                                     ).then_inc(mmB.h, 1)
                t.wait_ge(mmA.h, mmA.at(64 * 32))
                t.wait_ge(mmB.h, mmB.at(64 * 32))

            def f_scalar(s):
                for jj in range(32):
                    s.wait_ge(mmA.h, mmA.at(64 * (jj + 1)))
                    if jj >= 2:
                        s.wait_ge(vx4.h, vx4.at(jj - 1))
                    s.activation(g1b[:, jj % 2, :], psA[:, jj % 2, :], AF.Silu,
                                 bias=b12sb[:, jj:jj + 1]).then_inc(sg.h, 1)
                s.wait_ge(sg.h, sg.at(32))

            def f_vector(v):
                for jj in range(32):
                    v.wait_ge(mmB.h, mmB.at(64 * (jj + 1)))
                    if jj >= 2:
                        v.wait_ge(vx4.h, vx4.at(jj - 1))
                    v.tensor_scalar_add(x2bb[:, jj % 2, :], psB[:, jj % 2, :],
                                        b12sb[:, 32 + jj:33 + jj]).then_inc(vx2.h, 1)
                    v.wait_ge(sg.h, sg.at(jj + 1))
                    v.wait_ge(vx2.h, vx2.at(jj + 1))
                    v.tensor_mul(x4[:, jj, :], g1b[:, jj % 2, :], x2bb[:, jj % 2, :]
                                 ).then_inc(vx4.h, 1)
                v.wait_ge(vx2.h, vx2.at(32))
                v.wait_ge(vx4.h, vx4.at(32))

            run_phase(name, gpsimd=f_gp, tensor=f_tensor, scalar=f_scalar, vector=f_vector)
            wld[0].adv(16 * 32)
            wld[1].adv(16 * 32)
            mmA.adv(64 * 32)
            mmB.adv(64 * 32)
            sg.adv(32)
            vx2.adv(32)
            vx4.adv(32)

        def proj_out_phase(name, wt, src, bias, scale, resid, dst, wbuf, xcb, x1b,
                           f05b, skip_first=0, gp_extra=None):
            ps3 = globs["ps3"]
            residr = resid.rearrange("(c p) t -> c p t", p=128)
            dstr = dst.rearrange("(c p) t -> c p t", p=128)

            def f_gp(g):
                for j in range(NT3):
                    if j < skip_first:
                        continue
                    if j >= 2:
                        g.wait_ge(mm3.h, mm3.at(64 * (j - 1)))
                    g.dma_start(out=wbuf[:, j % 2], in_=wt[j]).then_inc(wld[j % 2].h, 16)
                g.wait_ge(wld[0].h, wld[0].at(16 * 16))
                g.wait_ge(wld[1].h, wld[1].at(16 * 16))
                if gp_extra:
                    gp_extra(g)

            def f_sync(sy):
                for j in range(NT3):
                    if j >= 2:
                        sy.wait_ge(vdone.h, vdone.at(j - 1))
                        sy.dma_start(out=dstr[j - 2], in_=x1b[:, j % 2, :]
                                     ).then_inc(xst[j % 2].h, 16)
                    sy.dma_start(out=xcb[:, j % 2, :], in_=residr[j]).then_inc(xld[j % 2].h, 16)
                for j in (NT3 - 2, NT3 - 1):
                    sy.wait_ge(vdone.h, vdone.at(j + 1))
                    sy.dma_start(out=dstr[j], in_=x1b[:, j % 2, :]).then_inc(xst[j % 2].h, 16)
                sy.wait_ge(xst[0].h, xst[0].at(16 * 16))
                sy.wait_ge(xst[1].h, xst[1].at(16 * 16))
                sy.wait_ge(xld[0].h, xld[0].at(16 * 16))
                sy.wait_ge(xld[1].h, xld[1].at(16 * 16))

            def f_tensor(t):
                for j in range(NT3):
                    t.wait_ge(wld[j % 2].h, wld[j % 2].at(16 * (j // 2 + 1)))
                    if j >= 2:
                        t.wait_ge(sf.h, sf.at(j - 1))
                    for c in range(NCH):
                        for hf in range(2):
                            cs = slice(512 * hf, 512 * hf + 512)
                            t.matmul(ps3[:, j % 2, cs], wbuf[:, j % 2, c, :],
                                     src[:, c, cs], start=(c == 0), stop=(c == NCH - 1)
                                     ).then_inc(mm3.h, 1)
                t.wait_ge(mm3.h, mm3.at(64 * 32))

            def f_scalar(s):
                for j in range(NT3):
                    s.wait_ge(mm3.h, mm3.at(64 * (j + 1)))
                    if j >= 2:
                        s.wait_ge(vdone.h, vdone.at(j - 1))
                    if bias is not None:
                        s.activation(f05b[:, j % 2, :], ps3[:, j % 2, :], AF.Identity,
                                     bias=bias[:, j:j + 1], scale=scale).then_inc(sf.h, 1)
                    else:
                        s.activation(f05b[:, j % 2, :], ps3[:, j % 2, :], AF.Copy
                                     ).then_inc(sf.h, 1)
                s.wait_ge(sf.h, sf.at(32))

            def f_vector(v):
                for j in range(NT3):
                    v.wait_ge(sf.h, sf.at(j + 1))
                    v.wait_ge(xld[j % 2].h, xld[j % 2].at(16 * (j // 2 + 1)))
                    if j >= 2:
                        v.wait_ge(xst[j % 2].h, xst[j % 2].at(16 * (j // 2)))
                    v.tensor_add(x1b[:, j % 2, :], f05b[:, j % 2, :], xcb[:, j % 2, :]
                                 ).then_inc(vdone.h, 1)
                v.wait_ge(vdone.h, vdone.at(32))

            run_phase(name, gpsimd=f_gp, sync=f_sync, tensor=f_tensor,
                      scalar=f_scalar, vector=f_vector)
            wld[0].adv(16 * 16)
            wld[1].adv(16 * 16)
            mm3.adv(64 * 32)
            sf.adv(NT3)
            vdone.adv(NT3)
            xld[0].adv(16 * 16)
            xld[1].adv(16 * 16)
            xst[0].adv(16 * 16)
            xst[1].adv(16 * 16)

        def qkv_phase(name, xln, q_all, k_all, v_fm, wbuf, qrot, qpl):
            psq = globs["psq"]
            rtmp1, rtmp2 = globs["rtmp1"], globs["rtmp2"]
            sinsb, cossb = globs["sinsb"], globs["cossb"]
            trw = globs["trw"]
            items = []
            for j in range(NT3):
                items.append((wq_t[j], q_all, j, True))
            for j in range(KVT):
                items.append((wk_t[j], k_all, j, True))
            for j in range(KVT):
                items.append((wv_t[j], v_fm, j, False))
            if _QKV_NROPE is not None:
                items = [(w, d, j, rope and i < _QKV_NROPE)
                         for i, (w, d, j, rope) in enumerate(items)]
            nit = len(items)
            nrope = sum(1 for it in items if it[3])

            def scnt(i):
                return sum(3 if items[k][3] else 1 for k in range(i + 1))

            def f_gp(g):
                for idx in range(nit):
                    if idx >= 2:
                        g.wait_ge(mmq.h, mmq.at(64 * (idx - 1)))
                    g.dma_start(out=wbuf[:, idx % 2], in_=items[idx][0]
                                ).then_inc(wld[idx % 2].h, 16)
                g.wait_ge(wld[0].h, wld[0].at(16 * 24))
                g.wait_ge(wld[1].h, wld[1].at(16 * 24))

            def f_sync(sy):
                sy.dma_start(out=sinsb[:, :], in_=argT[:, :]).then_inc(scs.h, 16)
                sy.dma_start(out=trw[:, :, :], in_=trwT[:, :, :]).then_inc(scs.h, 16)
                sy.wait_ge(scs.h, scs.at(32))

            def trig_vec(v):
                v.wait_ge(scs.h, scs.at(33))
                v.tensor_mul(cossb[:, :], rtmp2[:, :], rtmp2[:, :]).then_inc(scs.h, 1)
                v.wait_ge(scs.h, scs.at(36))
                v.tensor_mul(sinsb[:, :], rtmp2[:, :], trw[:, 0, :]).then_inc(scs.h, 1)
                v.wait_ge(scs.h, scs.at(37))
                v.tensor_mul(rtmp1[:, :], cossb[:, :], trw[:, 1, :]).then_inc(scs.h, 1)
                v.wait_ge(scs.h, scs.at(38))
                v.tensor_add(sinsb[:, :], sinsb[:, :], rtmp1[:, :]).then_inc(scs.h, 1)
                v.wait_ge(scs.h, scs.at(39))
                v.tensor_mul(rtmp1[:, :], rtmp2[:, :], trw[:, 2, :]).then_inc(scs.h, 1)
                v.wait_ge(scs.h, scs.at(40))
                v.tensor_mul(cossb[:, :], cossb[:, :], trw[:, 3, :]).then_inc(scs.h, 1)
                v.wait_ge(scs.h, scs.at(41))
                v.tensor_add(cossb[:, :], cossb[:, :], rtmp1[:, :]).then_inc(scs.h, 1)
                v.wait_ge(scs.h, scs.at(42))

            def f_tensor(t):
                for idx, (_, dstt, j, rope) in enumerate(items):
                    t.wait_ge(wld[idx % 2].h, wld[idx % 2].at(16 * (idx // 2 + 1)))
                    if idx >= 2:
                        t.wait_ge(sq.h, sq.at(scnt(idx - 2)))
                    for c in range(NCH):
                        for hf in range(2):
                            cs = slice(512 * hf, 512 * hf + 512)
                            t.matmul(psq[:, idx % 2, cs], wbuf[:, idx % 2, c, :],
                                     xln[:, c, cs], start=(c == 0), stop=(c == NCH - 1)
                                     ).then_inc(mmq.h, 1)
                t.wait_ge(mmq.h, mmq.at(64 * nit))

            def f_scalar(s):
                s.wait_ge(scs.h, scs.at(32))
                with nc.allow_low_precision(reason="rope coefficients are bf16-"
                                            "class matmul-side operands"):
                    s.activation(rtmp2[:, :], sinsb[:, :], AF.Sin).then_inc(scs.h, 1)
                s.wait_ge(scs.h, scs.at(34))
                s.activation(cossb[:, :], cossb[:, :], AF.Identity, bias=1.0,
                             scale=-1.0).then_inc(scs.h, 1)
                s.wait_ge(scs.h, scs.at(35))
                s.activation(cossb[:, :], cossb[:, :], AF.Sqrt).then_inc(scs.h, 1)
                for idx, (_, dstt, j, rope) in enumerate(items):
                    s.wait_ge(mmq.h, mmq.at(64 * (idx + 1)))
                    if rope:
                        r = idx
                        if r >= 2:
                            s.wait_ge(pfree.h, pfree.at(r - 1))
                            s.wait_ge(prot.h, prot.at(r - 1))
                        s.activation(qpl[:, idx % 2, :], psq[:, idx % 2, :],
                                     AF.Copy).then_inc(sq.h, 1)
                        s.activation(qrot[0:64, idx % 2, :], psq[64:128, idx % 2, :],
                                     AF.Copy).then_inc(sq.h, 1)
                        s.activation(qrot[64:128, idx % 2, :], psq[0:64, idx % 2, :],
                                     AF.Copy).then_inc(sq.h, 1)
                    else:
                        s.activation(dstt[:, :, j], psq[:, idx % 2, :], AF.Copy
                                     ).then_inc(sq.h, 1)
                s.wait_ge(sq.h, sq.at(scnt(nit - 1)))

            def f_vector(v):
                trig_vec(v)
                for idx, (_, dstt, j, rope) in enumerate(items):
                    if not rope:
                        continue
                    r = idx
                    v.wait_ge(sq.h, sq.at(scnt(idx)))
                    if r >= 1:
                        v.wait_ge(pq.h, pq.at(r))
                    v.tensor_mul(rtmp1[:, :], qpl[:, idx % 2, :], cossb[:, :]
                                 ).then_inc(pfree.h, 1)
                    v.tensor_mul(rtmp2[:, :], qrot[:, idx % 2, :], sinsb[:, :]
                                 ).then_inc(prot.h, 1)
                    v.wait_ge(pfree.h, pfree.at(r + 1))
                    v.wait_ge(prot.h, prot.at(r + 1))
                    v.tensor_add(dstt[:, :, j], rtmp1[:, :], rtmp2[:, :]
                                 ).then_inc(pq.h, 1)
                if nrope:
                    v.wait_ge(pq.h, pq.at(nrope))

            run_phase(name, gpsimd=f_gp, sync=f_sync, tensor=f_tensor,
                      scalar=f_scalar, vector=f_vector)
            wld[0].adv(16 * 24)
            wld[1].adv(16 * 24)
            mmq.adv(64 * nit)
            sq.adv(scnt(nit - 1))
            scs.adv(42)
            pfree.adv(nrope)
            prot.adv(nrope)
            pq.adv(nrope)

        def attn_phase(name, q_all, k_all, v_fm, attn_buf, ab, gp_extra=None):
            expb, phatb, vpackb, sumb, arbsb = ab
            ps_sc = globs["ps_sc"]
            ps_asum, ps_arb, ps_av = globs["ps_asum"], globs["ps_arb"], globs["ps_av"]
            N = NPACK

            def f_sync(sy):
                for w in range(N // 8):
                    i0 = 8 * w
                    if i0 >= 16:
                        sy.wait_ge(mmav.h, mmav.at(i0 - 8))
                    vsl = v_fm[:, 128 * w:128 * w + 128, :].rearrange("d t g -> d (t g)")
                    sy.dma_start_transpose(vpackb[:, i0 % 16:i0 % 16 + 8, :], vsl
                                           ).then_inc(svp.h, 16)
                    sy.wait_ge(svp.h, svp.at(16 * (w + 1)))

            def f_tensor(t):
                for k in range(N + 8):
                    if k < N:
                        i = k
                        if i >= 2:
                            t.wait_ge(exps.h, exps.at(i - 1))
                        t.matmul(ps_sc[:, i % 2, :], mA_bf[:, :], mB_bf[:, :],
                                 start=True, stop=False)
                        klh = k_all[:, 16 * i:16 * i + 16, :].rearrange("d t g -> d (t g)")
                        qrh = q_all[:, 16 * i:16 * i + 16, :].rearrange("d t h -> d (t h)")
                        t.matmul(ps_sc[:, i % 2, :], klh, qrh, start=False, stop=True
                                 ).then_inc(mmsc.h, 1)
                    if 0 <= k - 2 < N:
                        i = k - 2
                        t.wait_ge(exps.h, exps.at(i + 1))
                        if i >= 2:
                            t.wait_ge(vrec.h, vrec.at(i - 1))
                        t.matmul(ps_asum[0:1, i % 2, :], ones_bf[:, :], expb[:, i % 8, :],
                                 start=True, stop=True).then_inc(mmsum.h, 1)
                    if 0 <= k - 4 < N:
                        i = k - 4
                        t.wait_ge(vrec.h, vrec.at(i + 1))
                        if i >= 2:
                            t.wait_ge(sarb.h, sarb.at(i - 1))
                        t.matmul(ps_arb[:, i % 2, :], onesK1[:, :], sumb[0:1, i % 4, :],
                                 start=True, stop=True).then_inc(mmrb.h, 1)
                    if 0 <= k - 8 < N:
                        i = k - 8
                        t.wait_ge(vphat.h, vphat.at(i // 2 + 1))
                        t.wait_ge(svp.h, svp.at(16 * (i // 8 + 1)))
                        if i >= 2:
                            t.wait_ge(sattn.h, sattn.at(i - 1))
                        t.matmul(ps_av[:, i % 2, :], vpackb[:, i % 16, :],
                                 phatb[:, (i // 2) % 2, i % 2, :],
                                 start=True, stop=True).then_inc(mmav.h, 1)
                for s2 in (mmsc, mmsum, mmrb, mmav):
                    t.wait_ge(s2.h, s2.at(N))

            def f_scalar(s):
                for k in range(N + 9):
                    if k < N:
                        i = k
                        s.wait_ge(mmsc.h, mmsc.at(i + 1))
                        if i >= 8:
                            s.wait_ge(mmsum.h, mmsum.at(i - 7))
                            s.wait_ge(vphat.h, vphat.at(i // 2 - 3))
                        s.activation(expb[:, i % 8, :], ps_sc[:, i % 2, :], AF.Exp,
                                     scale=ISD).then_inc(exps.h, 1)
                    if 0 <= k - 5 < N:
                        i = k - 5
                        s.wait_ge(mmrb.h, mmrb.at(i + 1))
                        if i >= 4:
                            s.wait_ge(vphat.h, vphat.at(i // 2))
                        s.activation(arbsb[:, i % 4, :], ps_arb[:, i % 2, :], AF.Copy
                                     ).then_inc(sarb.h, 1)
                    if 0 <= k - 9 < N:
                        i = k - 9
                        s.wait_ge(mmav.h, mmav.at(i + 1))
                        s.activation(attn_buf[:, :, 16 * i:16 * i + 16],
                                     ps_av[:, i % 2, :].rearrange("d (t h) -> d h t", t=16),
                                     AF.Copy).then_inc(sattn.h, 1)
                for s2 in (exps, svp, sarb, sattn):
                    s.wait_ge(s2.h, s2.at(N))

            def f_vector(v):
                for w in range(N // 2 + 2):
                    if w < N // 2:
                        for i in (2 * w, 2 * w + 1):
                            v.wait_ge(mmsum.h, mmsum.at(i + 1))
                            if i >= 4:
                                v.wait_ge(mmrb.h, mmrb.at(i - 3))
                            with nc.allow_low_precision(reason="reciprocal feeds a "
                                                        "bf16 product chain"):
                                v.reciprocal(sumb[0:1, i % 4, :], ps_asum[0:1, i % 2, :]
                                             ).then_inc(vrec.h, 1)
                    if 0 <= w - 2 < N // 2:
                        wp = w - 2
                        i0 = 2 * wp
                        v.wait_ge(sarb.h, sarb.at(i0 + 2))
                        v.wait_ge(exps.h, exps.at(i0 + 2))
                        if wp >= 2:
                            v.wait_ge(mmav.h, mmav.at(i0 - 2))
                        v.tensor_mul(phatb[:, wp % 2, :, :],
                                     expb[:, i0 % 8:i0 % 8 + 2, :],
                                     arbsb[:, i0 % 4:i0 % 4 + 2, :]).then_inc(vphat.h, 1)
                v.wait_ge(vrec.h, vrec.at(N))
                v.wait_ge(vphat.h, vphat.at(N // 2))

            run_phase(name, sync=f_sync, tensor=f_tensor, scalar=f_scalar,
                      vector=f_vector, gpsimd=gp_extra)
            for s2 in (mmsc, mmsum, mmrb, mmav, exps, sattn, vrec, sarb):
                s2.adv(N)
            svp.adv(16 * (N // 8))
            vphat.adv(N // 2)

        def ln_scope(stk):
            ps_sum = stk.enter_context(nc.psum_tensor(_u("ps_sum"), [128, TOK], F32))
            ps_sumsq = stk.enter_context(nc.psum_tensor(_u("ps_sumsq"), [128, TOK], F32))
            ps_rb = stk.enter_context(nc.psum_tensor(_u("ps_rb"), [128, TOK], F32))
            ps_mb = stk.enter_context(nc.psum_tensor(_u("ps_mb"), [128, TOK], F32))
            lntmp = stk.enter_context(nc.sbuf_tensor(_u("lntmp"), [128, 2, 4, TOK], BF16))
            lnb = alloc_ln(stk)
            return ps_sum, ps_sumsq, ps_rb, ps_mb, lntmp, lnb

        def ffn_sublayer(tag, src_ln, resid, dst, ln_pre=None):
            with ExitStack() as stF:
                xln = stF.enter_context(nc.sbuf_tensor("xln" + tag, [128, NCH, TOK], BF16))
                w12buf = stF.enter_context(nc.sbuf_tensor("w12b" + tag, [128, 2, NCH, 128], BF16))

                def pf_w12(g):
                    g.dma_start(out=w12buf[:, 0], in_=w12_t[0]).then_inc(wld[0].h, 16)
                    g.dma_start(out=w12buf[:, 1], in_=w12_t[32]).then_inc(wld[1].h, 16)

                with ExitStack() as stL:
                    ps_sum, ps_sumsq, ps_rb, ps_mb, lntmp, lnb = ln_scope(stL)
                    globs.update(ps_sum=ps_sum, ps_sumsq=ps_sumsq, ps_rb=ps_rb,
                                 ps_mb=ps_mb, lntmp=lntmp)
                    ln_phase(tag + "1", src_ln, xln, lnb, pre=ln_pre, gp_extra=pf_w12)
                with ExitStack() as stM:
                    x4 = stM.enter_context(nc.sbuf_tensor("x4" + tag, [128, NCH, TOK], BF16))
                    g1b = stM.enter_context(nc.sbuf_tensor("g1b" + tag, [128, 2, TOK], BF16))
                    x2bb = stM.enter_context(nc.sbuf_tensor("x2b" + tag, [128, 2, TOK], BF16))
                    wbuf3 = stM.enter_context(nc.sbuf_tensor("w3b" + tag, [128, 2, NCH, 128], BF16))

                    def pf_w3(g):
                        g.dma_start(out=wbuf3[:, 0], in_=w3_t[0]).then_inc(wld[0].h, 16)
                        g.dma_start(out=wbuf3[:, 1], in_=w3_t[1]).then_inc(wld[1].h, 16)

                    with ExitStack() as stM2:
                        psA_ = stM2.enter_context(nc.psum_tensor(_u("psA"), [128, 2, TOK], F32))
                        psB_ = stM2.enter_context(nc.psum_tensor(_u("psB"), [128, 2, TOK], F32))
                        globs.update(psA=psA_, psB=psB_)
                        ffn_m_phase(tag + "2", xln, x4, w12buf, g1b, x2bb,
                                    skip_first=2, gp_extra=pf_w3)
                    with ExitStack() as stP:
                        ps3_ = stP.enter_context(nc.psum_tensor(_u("ps3"), [128, 2, TOK], F32))
                        xcb = stP.enter_context(nc.sbuf_tensor("xcb" + tag, [128, 2, TOK], F32))
                        x1b = stP.enter_context(nc.sbuf_tensor("x1b" + tag, [128, 2, TOK], F32))
                        f05b = stP.enter_context(nc.sbuf_tensor("f05" + tag, [128, 2, TOK], F32))
                        globs.update(ps3=ps3_)
                        proj_out_phase(tag + "3", w3_t, x4, b3h, 0.5, resid, dst,
                                       wbuf3, xcb, x1b, f05b, skip_first=2)

        ffn_sublayer("A", xT, xT, x1T, ln_pre=p0_pre)
        p0_adv()

        with ExitStack() as stB:
            q_all = stB.enter_context(nc.sbuf_tensor("q_all", [128, TOK, NT3], BF16))
            k_all = stB.enter_context(nc.sbuf_tensor("k_all", [128, TOK, KVT], BF16))
            v_fm = stB.enter_context(nc.sbuf_tensor("v_fm", [128, TOK, KVT], BF16))
            with ExitStack() as stB1:
                xln2 = stB1.enter_context(nc.sbuf_tensor("xln2", [128, NCH, TOK], BF16))
                with ExitStack() as stL:
                    ps_sum, ps_sumsq, ps_rb, ps_mb, lntmp, lnb = ln_scope(stL)
                    globs.update(ps_sum=ps_sum, ps_sumsq=ps_sumsq, ps_rb=ps_rb,
                                 ps_mb=ps_mb, lntmp=lntmp)
                    ln_phase("B1", x1T, xln2, lnb)
                with ExitStack() as stQ:
                    psq_ = stQ.enter_context(nc.psum_tensor(_u("psq"), [128, 2, TOK], F32))
                    wbufq = stQ.enter_context(nc.sbuf_tensor("wbufq", [128, 2, NCH, 128], BF16))
                    qrot = stQ.enter_context(nc.sbuf_tensor("qrot", [128, 2, TOK], BF16))
                    qpl = stQ.enter_context(nc.sbuf_tensor("qpl", [128, 2, TOK], BF16))
                    rtmp1_ = stQ.enter_context(nc.sbuf_tensor("rtmp1", [128, TOK], BF16))
                    rtmp2_ = stQ.enter_context(nc.sbuf_tensor("rtmp2", [128, TOK], BF16))
                    sinsb_ = stQ.enter_context(nc.sbuf_tensor("sinsb", [128, TOK], F32))
                    cossb_ = stQ.enter_context(nc.sbuf_tensor("cossb", [128, TOK], F32))
                    trw_ = stQ.enter_context(nc.sbuf_tensor("trw", [128, 4, TOK], BF16))
                    globs.update(psq=psq_, rtmp1=rtmp1_, rtmp2=rtmp2_,
                                 sinsb=sinsb_, cossb=cossb_, trw=trw_)
                    qkv_phase("B2", xln2, q_all, k_all, v_fm, wbufq, qrot, qpl)
            with ExitStack() as stB3:
                attn_buf = stB3.enter_context(nc.sbuf_tensor("attn_buf", [128, NT3, TOK], BF16))
                wbufo = stB3.enter_context(nc.sbuf_tensor("wbufo", [128, 2, NCH, 128], BF16))

                def pf_wo(g):
                    g.dma_start(out=wbufo[:, 0], in_=wo_t[0]).then_inc(wld[0].h, 16)
                    g.dma_start(out=wbufo[:, 1], in_=wo_t[1]).then_inc(wld[1].h, 16)

                with ExitStack() as stAT:
                    ps_sc_ = stAT.enter_context(nc.psum_tensor(_u("ps_sc"), [128, 2, 512], F32))
                    ps_asum_ = stAT.enter_context(nc.psum_tensor(_u("ps_asum"), [128, 2, 512], F32))
                    ps_arb_ = stAT.enter_context(nc.psum_tensor(_u("ps_arb"), [128, 2, 512], F32))
                    ps_av_ = stAT.enter_context(nc.psum_tensor(_u("ps_av"), [128, 2, 512], F32))
                    expb = stAT.enter_context(nc.sbuf_tensor("expb", [128, 8, 512], BF16))
                    phatb = stAT.enter_context(nc.sbuf_tensor("phatb", [128, 2, 2, 512], BF16))
                    vpackb = stAT.enter_context(nc.sbuf_tensor("vpackb", [128, 16, 128], BF16))
                    sumb = stAT.enter_context(nc.sbuf_tensor("sumb", [1, 4, 512], BF16))
                    arbsb = stAT.enter_context(nc.sbuf_tensor("arbsb", [128, 4, 512], BF16))
                    globs.update(ps_sc=ps_sc_, ps_asum=ps_asum_,
                                 ps_arb=ps_arb_, ps_av=ps_av_)
                    attn_phase("B3", q_all, k_all, v_fm, attn_buf,
                               (expb, phatb, vpackb, sumb, arbsb), gp_extra=pf_wo)
                with ExitStack() as stP:
                    ps3_ = stP.enter_context(nc.psum_tensor(_u("ps3"), [128, 2, TOK], F32))
                    xcb = stP.enter_context(nc.sbuf_tensor("xcbB", [128, 2, TOK], F32))
                    x1b = stP.enter_context(nc.sbuf_tensor("x1bB", [128, 2, TOK], F32))
                    f05B = stP.enter_context(nc.sbuf_tensor("f05B", [128, 2, TOK], F32))
                    globs.update(ps3=ps3_)
                    proj_out_phase("B4", wo_t, attn_buf, None, 1.0, x1T, x2T,
                                   wbufo, xcb, x1b, f05B, skip_first=2)

        ffn_sublayer("C", x2T, x2T, outT)

    return nc


_CACHE = {}


def _host_tables():
    half = 64
    i = np.arange(half, dtype=np.float64)
    emb = 10000.0 ** (2.0 * np.floor(i / 2.0) / half)
    pos = np.arange(2048, dtype=np.float64)
    fr = pos[:, None] * emb[None, :]
    def red2pi(x):
        return x - np.round(x / (2 * np.pi)) * 2 * np.pi
    a = red2pi(fr)
    s_t, c_t = np.sin(a), np.cos(a)
    sw = (np.abs(c_t) < np.abs(s_t))
    m = np.where(sw, red2pi(np.pi / 2 - a), a)
    sgn_s = np.where(s_t >= 0, 1.0, -1.0)
    sgn_c = np.where(c_t >= 0, 1.0, -1.0)
    swf = sw.astype(np.float64)
    w0h, w1h = (1 - swf), swf * sgn_s
    w2h, w3h = swf, (1 - swf) * sgn_c
    sigma = np.concatenate([-np.ones(64), np.ones(64)])
    arg_full = np.concatenate([m, m], axis=-1).astype(np.float32)
    W0 = (np.concatenate([w0h, w0h], -1) * sigma).astype(np.float32)
    W1 = (np.concatenate([w1h, w1h], -1) * sigma).astype(np.float32)
    W2 = np.concatenate([w2h, w2h], -1).astype(np.float32)
    W3 = np.concatenate([w3h, w3h], -1).astype(np.float32)
    trw_full = np.stack([W0, W1, W2, W3], axis=0)
    C = float(2 ** 30)
    mA = np.zeros((17, 128), dtype=np.float32)
    mB = np.zeros((17, 512), dtype=np.float32)
    mA[0, :] = 1.0
    mB[0, :] = -C
    for t in range(16):
        mA[1 + t, 8 * t:8 * t + 8] = 1.0
        mB[1 + t, 32 * t:32 * t + 32] = C
    ide = np.eye(128, dtype=np.float32)
    return arg_full, trw_full, mA, mB, ide


def kernel(L0_attn_hidden_states, L0_attn_Wq, L0_attn_Wk, L0_attn_Wv, L0_attn_Wo,
           L0_ffn_w12, L0_ffn_b12, L0_ffn_w3, L0_ffn_b3):
    x = np.asarray(L0_attn_hidden_states, dtype=np.float32)
    Wq = np.asarray(L0_attn_Wq, dtype=np.float32)
    Wk = np.asarray(L0_attn_Wk, dtype=np.float32)
    Wv = np.asarray(L0_attn_Wv, dtype=np.float32)
    Wo = np.asarray(L0_attn_Wo, dtype=np.float32)
    w12 = np.asarray(L0_ffn_w12, dtype=np.float32)
    b12 = np.asarray(L0_ffn_b12, dtype=np.float32)
    w3 = np.asarray(L0_ffn_w3, dtype=np.float32)
    b3 = np.asarray(L0_ffn_b3, dtype=np.float32)

    wq_t = np.ascontiguousarray(Wq.reshape(32, 128, 32, 128).transpose(2, 1, 0, 3))
    wk_t = np.ascontiguousarray(Wk.reshape(32, 128, 8, 128).transpose(2, 1, 0, 3))
    wv_t = np.ascontiguousarray(Wv.reshape(32, 128, 8, 128).transpose(2, 1, 0, 3))
    wo_t = np.ascontiguousarray(Wo.reshape(32, 128, 32, 128).transpose(2, 1, 0, 3))
    w12_t = np.ascontiguousarray(w12.reshape(64, 128, 32, 128).transpose(0, 3, 2, 1))
    w3_t = np.ascontiguousarray(w3.reshape(32, 128, 32, 128).transpose(0, 3, 2, 1))
    b12T = np.ascontiguousarray(b12.reshape(64, 128).T)
    b3T = np.ascontiguousarray(b3.reshape(32, 128).T)

    arg_full, trw_full, mA, mB, ide = _host_tables()

    xr = x.reshape(4, 2, 1024, 4096)
    in_maps = []
    for core in range(8):
        b, hh = core // 2, core % 2
        xTc = np.ascontiguousarray(xr[b, hh].T)
        p0 = hh * 1024
        in_maps.append({
            "xT": xTc,
            "wq_t": wq_t, "wk_t": wk_t, "wv_t": wv_t, "wo_t": wo_t,
            "w12_t": w12_t, "w3_t": w3_t,
            "b12T": b12T, "b3T": b3T,
            "argT": np.ascontiguousarray(arg_full[p0:p0 + 1024].T),
            "trwT": np.ascontiguousarray(
                trw_full[:, p0:p0 + 1024, :].transpose(2, 0, 1)
            ).astype(ml_dtypes.bfloat16),
            "maskA": mA, "maskB": mB, "idem": ide,
        })

    nc = _CACHE.get("nc")
    if nc is None:
        nc = build_program()
        _CACHE["nc"] = nc

    res = run_bass_kernel_spmd(nc, in_maps, core_ids=list(range(8)))

    out = np.empty((4, 2048, 4096), dtype=np.float32)
    for core in range(8):
        b, hh = core // 2, core % 2
        out[b, hh * 1024:(hh + 1) * 1024, :] = res.results[core]["outT"].T
    return out


# version 15 (final)
# speedup vs baseline: 1.0831x; 1.0033x over previous
import numpy as np
import ml_dtypes
from contextlib import ExitStack

import concourse.bass as bass
import concourse.mybir as mybir
from concourse.bass_utils import run_bass_kernel_spmd

F32 = mybir.dt.float32
BF16 = mybir.dt.bfloat16
AF = mybir.ActivationFunctionType
ALU = mybir.AluOpType

TOK = 1024
NCH = 32
MT12 = 64
NT3 = 32
KVT = 8
NPACK = 64
ISD = 1.0 / float(np.float32(np.sqrt(np.float32(128.0))))
EPS = 1e-5


class Sem:
    def __init__(self, stack, nc, name):
        self.h = stack.enter_context(nc.semaphore(name))
        self.base = 0

    def at(self, k):
        return self.base + k

    def adv(self, n):
        self.base += n


_QKV_NROPE = None


def build_program(upto=None):
    nc = bass.Bass("TRN2", target_bir_lowering=False)

    xT = nc.dram_tensor("xT", [4096, TOK], F32, kind="ExternalInput").ap()
    wq_t = nc.dram_tensor("wq_t", [NT3, 128, NCH, 128], F32, kind="ExternalInput").ap()
    wk_t = nc.dram_tensor("wk_t", [KVT, 128, NCH, 128], F32, kind="ExternalInput").ap()
    wv_t = nc.dram_tensor("wv_t", [KVT, 128, NCH, 128], F32, kind="ExternalInput").ap()
    wo_t = nc.dram_tensor("wo_t", [NT3, 128, NCH, 128], F32, kind="ExternalInput").ap()
    w12_t = nc.dram_tensor("w12_t", [MT12, 128, NCH, 128], F32, kind="ExternalInput").ap()
    w3_t = nc.dram_tensor("w3_t", [NT3, 128, NCH, 128], F32, kind="ExternalInput").ap()
    b12T = nc.dram_tensor("b12T", [128, 64], F32, kind="ExternalInput").ap()
    b3T = nc.dram_tensor("b3T", [128, 32], F32, kind="ExternalInput").ap()
    sinT = nc.dram_tensor("sinT", [128, TOK], F32, kind="ExternalInput").ap()
    cosT = nc.dram_tensor("cosT", [128, TOK], F32, kind="ExternalInput").ap()
    maskA = nc.dram_tensor("maskA", [17, 128], F32, kind="ExternalInput").ap()
    maskB = nc.dram_tensor("maskB", [17, 512], F32, kind="ExternalInput").ap()
    idem = nc.dram_tensor("idem", [128, 128], F32, kind="ExternalInput").ap()
    outT = nc.dram_tensor("outT", [4096, TOK], F32, kind="ExternalOutput").ap()
    x1T = nc.dram_tensor("x1T", [4096, TOK], F32).ap()
    x2T = nc.dram_tensor("x2T", [4096, TOK], F32).ap()

    st = ExitStack()
    with st:
        b12sb = st.enter_context(nc.sbuf_tensor("b12sb", [128, 64], F32))
        b3sb = st.enter_context(nc.sbuf_tensor("b3sb", [128, 32], F32))
        b3h = st.enter_context(nc.sbuf_tensor("b3h", [128, 32], F32))
        mA_bf = st.enter_context(nc.sbuf_tensor("mA_bf", [17, 128], BF16))
        mB_bf = st.enter_context(nc.sbuf_tensor("mB_bf", [17, 512], BF16))
        ide_bf = st.enter_context(nc.sbuf_tensor("ide_bf", [128, 128], BF16))
        ones_bf = st.enter_context(nc.sbuf_tensor("ones_bf", [128, 1], BF16))
        onesK1 = st.enter_context(nc.sbuf_tensor("onesK1", [1, 128], BF16))
        onesK1f = st.enter_context(nc.sbuf_tensor("onesK1f", [1, 128], F32))

        bar = Sem(st, nc, "bar")
        nbar = [0]
        p0s = Sem(st, nc, "p0s")
        p0g = Sem(st, nc, "p0g")
        p0v = Sem(st, nc, "p0v")
        p0c = Sem(st, nc, "p0c")
        wld = [Sem(st, nc, "wld0"), Sem(st, nc, "wld1")]
        lnx = [Sem(st, nc, f"lnx{i}") for i in range(4)]
        lns = Sem(st, nc, "lns")
        lnmm = Sem(st, nc, "lnmm")
        lnvs = Sem(st, nc, "lnvs")
        lnve = Sem(st, nc, "lnve")
        lnss = Sem(st, nc, "lnss")
        lnrs = Sem(st, nc, "lnrs")
        lnbc = Sem(st, nc, "lnbc")
        lnsb = Sem(st, nc, "lnsb")
        lnafb = Sem(st, nc, "lnafb")
        lnafw = Sem(st, nc, "lnafw")
        mmA = Sem(st, nc, "mmA")
        mmB = Sem(st, nc, "mmB")
        sg = Sem(st, nc, "sg")
        vx2 = Sem(st, nc, "vx2")
        vx4 = Sem(st, nc, "vx4")
        xld = [Sem(st, nc, "xld0"), Sem(st, nc, "xld1")]
        xst = [Sem(st, nc, "xst0"), Sem(st, nc, "xst1")]
        vdone = Sem(st, nc, "vdone")
        mm3 = Sem(st, nc, "mm3")
        sf = Sem(st, nc, "sf")
        mmq = Sem(st, nc, "mmq")
        sq = Sem(st, nc, "sq")
        scs = Sem(st, nc, "scs")
        pfree = Sem(st, nc, "pfree")
        prot = Sem(st, nc, "prot")
        pq = Sem(st, nc, "pq")
        mmsc = Sem(st, nc, "mmsc")
        mmvp = Sem(st, nc, "mmvp")
        mmsum = Sem(st, nc, "mmsum")
        mmrb = Sem(st, nc, "mmrb")
        mmav = Sem(st, nc, "mmav")
        exps = Sem(st, nc, "exps")
        svp = Sem(st, nc, "svp")
        sattn = Sem(st, nc, "sattn")
        vrec = Sem(st, nc, "vrec")
        vphat = Sem(st, nc, "vphat")
        sarb = Sem(st, nc, "sarb")

        ENGS = ("gpsimd", "sync", "tensor", "vector", "scalar")
        globs = {}

        def run_phase(name, **bodies):
            nbar[0] += 1
            if upto is not None and nbar[0] > upto:
                return
            target = 5 * (nbar[0] - 1)
            with nc.Block(name) as blk:
                for en in ENGS:
                    f = bodies.get(en)

                    def mk(en=en, f=f):
                        def body(eng):
                            if target > 0:
                                eng.wait_ge(bar.h, target)
                            if f is not None:
                                f(eng)
                            eng.sem_inc(bar.h, 1)
                        return body

                    getattr(blk, en)(mk())

        def p0_sync(sy):
            sy.dma_start(out=b12sb[:, :], in_=b12T[:, :]).then_inc(p0s.h, 16)
            sy.dma_start(out=b3sb[:, :], in_=b3T[:, :]).then_inc(p0s.h, 16)
            sy.wait_ge(p0s.h, p0s.at(32))

        def p0_gp(g):
            g.dma_start(out=mA_bf[:, :], in_=maskA[:, :]).then_inc(p0g.h, 16)
            g.dma_start(out=mB_bf[:, :], in_=maskB[:, :]).then_inc(p0g.h, 16)
            g.dma_start(out=ide_bf[:, :], in_=idem[:, :]).then_inc(p0g.h, 16)
            g.wait_ge(p0g.h, p0g.at(48))

        def p0_vec(v):
            v.memset(ones_bf[:, :], 1.0).then_inc(p0v.h, 1)
            v.memset(onesK1[:, :], 1.0).then_inc(p0v.h, 1)
            v.memset(onesK1f[:, :], 1.0).then_inc(p0v.h, 1)
            v.wait_ge(p0v.h, p0v.at(3))

        def p0_scl(s):
            s.wait_ge(p0s.h, p0s.at(32))
            s.mul(b3h[:, :], b3sb[:, :], 0.5).then_inc(p0c.h, 1)
            s.wait_ge(p0c.h, p0c.at(1))

        p0_pre = dict(sync=p0_sync, gpsimd=p0_gp, vector=p0_vec, scalar=p0_scl)

        def p0_adv():
            p0s.adv(32)
            p0g.adv(48)
            p0v.adv(3)
            p0c.adv(1)

        def ln_phase(name, src, xln, lnb, pre=None, gp_extra=None):
            srcr = src.rearrange("(c p) t -> p c t", p=128)
            sqb, stt, rbb, mbb = lnb
            mu, s1, s2, s3 = stt
            ex2, rstd = s1, s1
            musq, sstd = s2, s2
            vareps, negmurs = s3, s3
            ps_sum, ps_sumsq = globs["ps_sum"], globs["ps_sumsq"]
            ps_rb, ps_mb, lntmp = globs["ps_rb"], globs["ps_mb"], globs["lntmp"]
            NB = 32

            def f_gp(g):
                if pre and pre.get("gpsimd"):
                    pre["gpsimd"](g)
                for b in range(NB):
                    if b >= 4:
                        g.wait_ge(lns.h, lns.at(b - 3))
                    g.dma_start(out=xln[:, b, :], in_=srcr[:, b, :]
                                ).then_inc(lnx[b % 4].h, 16)
                if gp_extra:
                    gp_extra(g)
                for p in range(4):
                    g.wait_ge(lnx[p].h, lnx[p].at(16 * 8))

            def f_sync(sy):
                if pre and pre.get("sync"):
                    pre["sync"](sy)

            def f_vector(v):
                if pre and pre.get("vector"):
                    pre["vector"](v)
                v.wait_ge(lnmm.h, lnmm.at(4 * NB))
                v.tensor_scalar_mul(mu[0:1, :], ps_sum[0:1, :], 1.0 / 4096.0
                                    ).then_inc(lnvs.h, 1)
                v.tensor_scalar_mul(ex2[0:1, :], ps_sumsq[0:1, :], 1.0 / 4096.0
                                    ).then_inc(lnvs.h, 1)
                v.wait_ge(lnvs.h, lnvs.at(2))
                v.tensor_mul(musq[0:1, :], mu[0:1, :], mu[0:1, :]).then_inc(lnvs.h, 1)
                v.wait_ge(lnvs.h, lnvs.at(3))
                v.scalar_tensor_tensor(vareps[0:1, :], ex2[0:1, :], EPS, musq[0:1, :],
                                       ALU.add, ALU.subtract).then_inc(lnve.h, 1)
                v.wait_ge(lnss.h, lnss.at(1))
                v.wait_ge(lnve.h, lnve.at(1))
                v.reciprocal(rstd[0:1, :], sstd[0:1, :]).then_inc(lnvs.h, 1)
                v.wait_ge(lnvs.h, lnvs.at(4))
                v.scalar_tensor_tensor(negmurs[0:1, :], mu[0:1, :], -1.0, rstd[0:1, :],
                                       ALU.mult, ALU.mult).then_inc(lnrs.h, 1)
                rbB = rbb.rearrange("p (a t) -> p a t", a=1).broadcast_to((128, 4, TOK))
                mbB = mbb.rearrange("p (a t) -> p a t", a=1).broadcast_to((128, 4, TOK))

                def aff1(g):
                    if g >= 2:
                        v.wait_ge(lnafw.h, lnafw.at(g - 1))
                    v.tensor_mul(lntmp[:, g % 2], xln[:, 4 * g:4 * g + 4, :], rbB
                                 ).then_inc(lnafb.h, 1)

                v.wait_ge(lnsb.h, lnsb.at(2))
                aff1(0)
                for g in range(8):
                    if g + 1 < 8:
                        aff1(g + 1)
                    v.wait_ge(lnafb.h, lnafb.at(g + 1))
                    v.tensor_add(xln[:, 4 * g:4 * g + 4, :], lntmp[:, g % 2], mbB
                                 ).then_inc(lnafw.h, 1)
                v.wait_ge(lnrs.h, lnrs.at(1))
                v.wait_ge(lnafw.h, lnafw.at(8))

            def f_scalar(s):
                if pre and pre.get("scalar"):
                    pre["scalar"](s)
                for b in range(NB):
                    s.wait_ge(lnx[b % 4].h, lnx[b % 4].at(16 * (b // 4 + 1)))
                    if b >= 2:
                        s.wait_ge(lnmm.h, lnmm.at(4 * (b - 1)))
                    s.activation(sqb[:, b % 2, :], xln[:, b, :], AF.Square
                                 ).then_inc(lns.h, 1)
                s.wait_ge(lnve.h, lnve.at(1))
                s.wait_ge(lnvs.h, lnvs.at(3))
                s.activation(sstd[0:1, :], vareps[0:1, :], AF.Sqrt).then_inc(lnss.h, 1)
                s.wait_ge(lnbc.h, lnbc.at(4))
                s.activation(rbb[:, :], ps_rb[:, :], AF.Copy).then_inc(lnsb.h, 1)
                s.activation(mbb[:, :], ps_mb[:, :], AF.Copy).then_inc(lnsb.h, 1)
                s.wait_ge(lnsb.h, lnsb.at(2))

            def f_tensor(t):
                if pre:
                    t.wait_ge(p0v.h, p0v.at(3))
                for b in range(NB):
                    t.wait_ge(lnx[b % 4].h, lnx[b % 4].at(16 * (b // 4 + 1)))
                    t.wait_ge(lns.h, lns.at(b + 1))
                    first = (b == 0)
                    last = (b == NB - 1)
                    for hf in range(2):
                        cs = slice(512 * hf, 512 * hf + 512)
                        t.matmul(ps_sum[0:1, cs], ones_bf[:, :], xln[:, b, cs],
                                 start=first, stop=last).then_inc(lnmm.h, 1)
                        t.matmul(ps_sumsq[0:1, cs], ones_bf[:, :], sqb[:, b % 2, cs],
                                 start=first, stop=last).then_inc(lnmm.h, 1)
                t.wait_ge(lnrs.h, lnrs.at(1))
                t.wait_ge(lnvs.h, lnvs.at(4))
                for hf in range(2):
                    cs = slice(512 * hf, 512 * hf + 512)
                    t.matmul(ps_rb[:, cs], onesK1f[:, :], rstd[0:1, cs], start=True,
                             stop=True).then_inc(lnbc.h, 1)
                    t.matmul(ps_mb[:, cs], onesK1f[:, :], negmurs[0:1, cs], start=True,
                             stop=True).then_inc(lnbc.h, 1)
                t.wait_ge(lnmm.h, lnmm.at(4 * NB))
                t.wait_ge(lnbc.h, lnbc.at(4))

            run_phase(name, gpsimd=f_gp, sync=f_sync, vector=f_vector,
                      scalar=f_scalar, tensor=f_tensor)
            for s2 in lnx:
                s2.adv(16 * 8)
            lns.adv(NB)
            lnmm.adv(4 * NB)
            lnvs.adv(4)
            for s2 in (lnve, lnss, lnrs):
                s2.adv(1)
            lnbc.adv(4)
            lnsb.adv(2)
            lnafb.adv(8)
            lnafw.adv(8)

        uid = [0]

        def _u(base):
            uid[0] += 1
            return f"{base}_{uid[0]}"

        def alloc_ln(stk):
            sqb = stk.enter_context(nc.sbuf_tensor(_u("lnsq"), [128, 2, TOK], BF16))
            stt = [stk.enter_context(nc.sbuf_tensor(_u("lnst"), [1, TOK], F32)) for i in range(4)]
            rbb = stk.enter_context(nc.sbuf_tensor(_u("lnrbb"), [128, TOK], BF16))
            mbb = stk.enter_context(nc.sbuf_tensor(_u("lnmbb"), [128, TOK], BF16))
            return (sqb, stt, rbb, mbb)

        def ffn_m_phase(name, xln, x4, w12buf, g1b, x2bb, skip_first=0, gp_extra=None):
            psA, psB = globs["psA"], globs["psB"]
            seq = []
            for jj in range(32):
                seq.append(jj)
                seq.append(jj + 32)

            def f_gp(g):
                for idx, j in enumerate(seq):
                    if idx < skip_first:
                        continue
                    if idx >= 2:
                        if idx % 2 == 0:
                            g.wait_ge(mmA.h, mmA.at(64 * (idx // 2)))
                        else:
                            g.wait_ge(mmB.h, mmB.at(64 * (idx // 2)))
                    g.dma_start(out=w12buf[:, idx % 2], in_=w12_t[j]
                                ).then_inc(wld[idx % 2].h, 16)
                g.wait_ge(wld[0].h, wld[0].at(16 * 32))
                g.wait_ge(wld[1].h, wld[1].at(16 * 32))
                if gp_extra:
                    gp_extra(g)

            def f_tensor(t):
                for jj in range(32):
                    t.wait_ge(wld[0].h, wld[0].at(16 * (jj + 1)))
                    if jj >= 2:
                        t.wait_ge(sg.h, sg.at(jj - 1))
                    for c in range(NCH):
                        for hf in range(2):
                            cs = slice(512 * hf, 512 * hf + 512)
                            t.matmul(psA[:, jj % 2, cs], w12buf[:, 0, c, :],
                                     xln[:, c, cs], start=(c == 0), stop=(c == NCH - 1)
                                     ).then_inc(mmA.h, 1)
                    t.wait_ge(wld[1].h, wld[1].at(16 * (jj + 1)))
                    if jj >= 2:
                        t.wait_ge(vx2.h, vx2.at(jj - 1))
                    for c in range(NCH):
                        for hf in range(2):
                            cs = slice(512 * hf, 512 * hf + 512)
                            t.matmul(psB[:, jj % 2, cs], w12buf[:, 1, c, :],
                                     xln[:, c, cs], start=(c == 0), stop=(c == NCH - 1)
                                     ).then_inc(mmB.h, 1)
                t.wait_ge(mmA.h, mmA.at(64 * 32))
                t.wait_ge(mmB.h, mmB.at(64 * 32))

            def f_scalar(s):
                for jj in range(32):
                    s.wait_ge(mmA.h, mmA.at(64 * (jj + 1)))
                    if jj >= 2:
                        s.wait_ge(vx4.h, vx4.at(jj - 1))
                    s.activation(g1b[:, jj % 2, :], psA[:, jj % 2, :], AF.Silu,
                                 bias=b12sb[:, jj:jj + 1]).then_inc(sg.h, 1)
                s.wait_ge(sg.h, sg.at(32))

            def f_vector(v):
                for jj in range(32):
                    v.wait_ge(mmB.h, mmB.at(64 * (jj + 1)))
                    if jj >= 2:
                        v.wait_ge(vx4.h, vx4.at(jj - 1))
                    v.tensor_scalar_add(x2bb[:, jj % 2, :], psB[:, jj % 2, :],
                                        b12sb[:, 32 + jj:33 + jj]).then_inc(vx2.h, 1)
                    v.wait_ge(sg.h, sg.at(jj + 1))
                    v.wait_ge(vx2.h, vx2.at(jj + 1))
                    v.tensor_mul(x4[:, jj, :], g1b[:, jj % 2, :], x2bb[:, jj % 2, :]
                                 ).then_inc(vx4.h, 1)
                v.wait_ge(vx2.h, vx2.at(32))
                v.wait_ge(vx4.h, vx4.at(32))

            run_phase(name, gpsimd=f_gp, tensor=f_tensor, scalar=f_scalar, vector=f_vector)
            wld[0].adv(16 * 32)
            wld[1].adv(16 * 32)
            mmA.adv(64 * 32)
            mmB.adv(64 * 32)
            sg.adv(32)
            vx2.adv(32)
            vx4.adv(32)

        def proj_out_phase(name, wt, src, bias, scale, resid, dst, wbuf, xcb, x1b,
                           f05b, skip_first=0, gp_extra=None):
            ps3 = globs["ps3"]
            residr = resid.rearrange("(c p) t -> c p t", p=128)
            dstr = dst.rearrange("(c p) t -> c p t", p=128)

            def f_gp(g):
                for j in range(NT3):
                    if j < skip_first:
                        continue
                    if j >= 2:
                        g.wait_ge(mm3.h, mm3.at(64 * (j - 1)))
                    g.dma_start(out=wbuf[:, j % 2], in_=wt[j]).then_inc(wld[j % 2].h, 16)
                g.wait_ge(wld[0].h, wld[0].at(16 * 16))
                g.wait_ge(wld[1].h, wld[1].at(16 * 16))
                if gp_extra:
                    gp_extra(g)

            def f_sync(sy):
                for j in range(NT3):
                    if j >= 2:
                        sy.wait_ge(vdone.h, vdone.at(j - 1))
                        sy.dma_start(out=dstr[j - 2], in_=x1b[:, j % 2, :]
                                     ).then_inc(xst[j % 2].h, 16)
                    sy.dma_start(out=xcb[:, j % 2, :], in_=residr[j]).then_inc(xld[j % 2].h, 16)
                for j in (NT3 - 2, NT3 - 1):
                    sy.wait_ge(vdone.h, vdone.at(j + 1))
                    sy.dma_start(out=dstr[j], in_=x1b[:, j % 2, :]).then_inc(xst[j % 2].h, 16)
                sy.wait_ge(xst[0].h, xst[0].at(16 * 16))
                sy.wait_ge(xst[1].h, xst[1].at(16 * 16))
                sy.wait_ge(xld[0].h, xld[0].at(16 * 16))
                sy.wait_ge(xld[1].h, xld[1].at(16 * 16))

            def f_tensor(t):
                for j in range(NT3):
                    t.wait_ge(wld[j % 2].h, wld[j % 2].at(16 * (j // 2 + 1)))
                    if j >= 2:
                        t.wait_ge(sf.h, sf.at(j - 1))
                    for c in range(NCH):
                        for hf in range(2):
                            cs = slice(512 * hf, 512 * hf + 512)
                            t.matmul(ps3[:, j % 2, cs], wbuf[:, j % 2, c, :],
                                     src[:, c, cs], start=(c == 0), stop=(c == NCH - 1)
                                     ).then_inc(mm3.h, 1)
                t.wait_ge(mm3.h, mm3.at(64 * 32))

            def f_scalar(s):
                for j in range(NT3):
                    s.wait_ge(mm3.h, mm3.at(64 * (j + 1)))
                    if j >= 2:
                        s.wait_ge(vdone.h, vdone.at(j - 1))
                    if bias is not None:
                        s.activation(f05b[:, j % 2, :], ps3[:, j % 2, :], AF.Identity,
                                     bias=bias[:, j:j + 1], scale=scale).then_inc(sf.h, 1)
                    else:
                        s.activation(f05b[:, j % 2, :], ps3[:, j % 2, :], AF.Copy
                                     ).then_inc(sf.h, 1)
                s.wait_ge(sf.h, sf.at(32))

            def f_vector(v):
                for j in range(NT3):
                    v.wait_ge(sf.h, sf.at(j + 1))
                    v.wait_ge(xld[j % 2].h, xld[j % 2].at(16 * (j // 2 + 1)))
                    if j >= 2:
                        v.wait_ge(xst[j % 2].h, xst[j % 2].at(16 * (j // 2)))
                    v.tensor_add(x1b[:, j % 2, :], f05b[:, j % 2, :], xcb[:, j % 2, :]
                                 ).then_inc(vdone.h, 1)
                v.wait_ge(vdone.h, vdone.at(32))

            run_phase(name, gpsimd=f_gp, sync=f_sync, tensor=f_tensor,
                      scalar=f_scalar, vector=f_vector)
            wld[0].adv(16 * 16)
            wld[1].adv(16 * 16)
            mm3.adv(64 * 32)
            sf.adv(NT3)
            vdone.adv(NT3)
            xld[0].adv(16 * 16)
            xld[1].adv(16 * 16)
            xst[0].adv(16 * 16)
            xst[1].adv(16 * 16)

        def qkv_phase(name, xln, q_all, k_all, v_fm, wbuf, qrot, qpl):
            psq = globs["psq"]
            rtmp1, rtmp2 = globs["rtmp1"], globs["rtmp2"]
            sinsb, cossb = globs["sinsb"], globs["cossb"]
            items = []
            for j in range(NT3):
                items.append((wq_t[j], q_all, j, True))
            for j in range(KVT):
                items.append((wk_t[j], k_all, j, True))
            for j in range(KVT):
                items.append((wv_t[j], v_fm, j, False))
            if _QKV_NROPE is not None:
                items = [(w, d, j, rope and i < _QKV_NROPE)
                         for i, (w, d, j, rope) in enumerate(items)]
            nit = len(items)
            nrope = sum(1 for it in items if it[3])

            def scnt(i):
                return sum(3 if items[k][3] else 1 for k in range(i + 1))

            def f_gp(g):
                for idx in range(nit):
                    if idx >= 2:
                        g.wait_ge(mmq.h, mmq.at(64 * (idx - 1)))
                    g.dma_start(out=wbuf[:, idx % 2], in_=items[idx][0]
                                ).then_inc(wld[idx % 2].h, 16)
                g.wait_ge(wld[0].h, wld[0].at(16 * 24))
                g.wait_ge(wld[1].h, wld[1].at(16 * 24))

            def f_sync(sy):
                sy.dma_start(out=sinsb[:, :], in_=sinT[:, :]).then_inc(scs.h, 16)
                sy.dma_start(out=cossb[:, :], in_=cosT[:, :]).then_inc(scs.h, 16)
                sy.wait_ge(scs.h, scs.at(32))

            def f_tensor(t):
                for idx, (_, dstt, j, rope) in enumerate(items):
                    t.wait_ge(wld[idx % 2].h, wld[idx % 2].at(16 * (idx // 2 + 1)))
                    if idx >= 2:
                        t.wait_ge(sq.h, sq.at(scnt(idx - 2)))
                    for c in range(NCH):
                        for hf in range(2):
                            cs = slice(512 * hf, 512 * hf + 512)
                            t.matmul(psq[:, idx % 2, cs], wbuf[:, idx % 2, c, :],
                                     xln[:, c, cs], start=(c == 0), stop=(c == NCH - 1)
                                     ).then_inc(mmq.h, 1)
                t.wait_ge(mmq.h, mmq.at(64 * nit))

            def f_scalar(s):
                for idx, (_, dstt, j, rope) in enumerate(items):
                    s.wait_ge(mmq.h, mmq.at(64 * (idx + 1)))
                    if rope:
                        r = idx
                        if r >= 2:
                            s.wait_ge(pfree.h, pfree.at(r - 1))
                            s.wait_ge(prot.h, prot.at(r - 1))
                        s.activation(qpl[:, idx % 2, :], psq[:, idx % 2, :],
                                     AF.Copy).then_inc(sq.h, 1)
                        s.activation(qrot[0:64, idx % 2, :], psq[64:128, idx % 2, :],
                                     AF.Copy).then_inc(sq.h, 1)
                        s.activation(qrot[64:128, idx % 2, :], psq[0:64, idx % 2, :],
                                     AF.Copy).then_inc(sq.h, 1)
                    else:
                        s.activation(dstt[:, :, j], psq[:, idx % 2, :], AF.Copy
                                     ).then_inc(sq.h, 1)
                s.wait_ge(sq.h, sq.at(scnt(nit - 1)))

            def f_vector(v):
                v.wait_ge(scs.h, scs.at(32))
                for idx, (_, dstt, j, rope) in enumerate(items):
                    if not rope:
                        continue
                    r = idx
                    v.wait_ge(sq.h, sq.at(scnt(idx)))
                    if r >= 1:
                        v.wait_ge(pq.h, pq.at(r))
                    v.tensor_mul(rtmp1[:, :], qpl[:, idx % 2, :], cossb[:, :]
                                 ).then_inc(pfree.h, 1)
                    v.tensor_mul(rtmp2[:, :], qrot[:, idx % 2, :], sinsb[:, :]
                                 ).then_inc(prot.h, 1)
                    v.wait_ge(pfree.h, pfree.at(r + 1))
                    v.wait_ge(prot.h, prot.at(r + 1))
                    v.tensor_add(dstt[:, :, j], rtmp1[:, :], rtmp2[:, :]
                                 ).then_inc(pq.h, 1)
                if nrope:
                    v.wait_ge(pq.h, pq.at(nrope))

            run_phase(name, gpsimd=f_gp, sync=f_sync, tensor=f_tensor,
                      scalar=f_scalar, vector=f_vector)
            wld[0].adv(16 * 24)
            wld[1].adv(16 * 24)
            mmq.adv(64 * nit)
            sq.adv(scnt(nit - 1))
            scs.adv(32)
            pfree.adv(nrope)
            prot.adv(nrope)
            pq.adv(nrope)

        def attn_phase(name, q_all, k_all, v_fm, attn_buf, ab, gp_extra=None):
            expb, phatb, vpackb, sumb, arbsb = ab
            ps_sc = globs["ps_sc"]
            ps_asum, ps_arb, ps_av = globs["ps_asum"], globs["ps_arb"], globs["ps_av"]
            N = NPACK

            def f_sync(sy):
                for w in range(N // 8):
                    i0 = 8 * w
                    if i0 >= 16:
                        sy.wait_ge(mmav.h, mmav.at(i0 - 8))
                    vsl = v_fm[:, 128 * w:128 * w + 128, :].rearrange("d t g -> d (t g)")
                    sy.dma_start_transpose(vpackb[:, i0 % 16:i0 % 16 + 8, :], vsl
                                           ).then_inc(svp.h, 16)
                    sy.wait_ge(svp.h, svp.at(16 * (w + 1)))

            def f_tensor(t):
                for k in range(N + 8):
                    if k < N:
                        i = k
                        if i >= 2:
                            t.wait_ge(exps.h, exps.at(i - 1))
                        t.matmul(ps_sc[:, i % 2, :], mA_bf[:, :], mB_bf[:, :],
                                 start=True, stop=False)
                        klh = k_all[:, 16 * i:16 * i + 16, :].rearrange("d t g -> d (t g)")
                        qrh = q_all[:, 16 * i:16 * i + 16, :].rearrange("d t h -> d (t h)")
                        t.matmul(ps_sc[:, i % 2, :], klh, qrh, start=False, stop=True
                                 ).then_inc(mmsc.h, 1)
                    if 0 <= k - 2 < N:
                        i = k - 2
                        t.wait_ge(exps.h, exps.at(i + 1))
                        if i >= 2:
                            t.wait_ge(vrec.h, vrec.at(i - 1))
                        t.matmul(ps_asum[0:1, i % 2, :], ones_bf[:, :], expb[:, i % 8, :],
                                 start=True, stop=True).then_inc(mmsum.h, 1)
                    if 0 <= k - 4 < N:
                        i = k - 4
                        t.wait_ge(vrec.h, vrec.at(i + 1))
                        if i >= 2:
                            t.wait_ge(sarb.h, sarb.at(i - 1))
                        t.matmul(ps_arb[:, i % 2, :], onesK1[:, :], sumb[0:1, i % 4, :],
                                 start=True, stop=True).then_inc(mmrb.h, 1)
                    if 0 <= k - 8 < N:
                        i = k - 8
                        t.wait_ge(vphat.h, vphat.at(i // 2 + 1))
                        t.wait_ge(svp.h, svp.at(16 * (i // 8 + 1)))
                        if i >= 2:
                            t.wait_ge(sattn.h, sattn.at(i - 1))
                        t.matmul(ps_av[:, i % 2, :], vpackb[:, i % 16, :],
                                 phatb[:, (i // 2) % 2, i % 2, :],
                                 start=True, stop=True).then_inc(mmav.h, 1)
                for s2 in (mmsc, mmsum, mmrb, mmav):
                    t.wait_ge(s2.h, s2.at(N))

            def f_scalar(s):
                for k in range(N + 9):
                    if k < N:
                        i = k
                        s.wait_ge(mmsc.h, mmsc.at(i + 1))
                        if i >= 8:
                            s.wait_ge(mmsum.h, mmsum.at(i - 7))
                            s.wait_ge(vphat.h, vphat.at(i // 2 - 3))
                        s.activation(expb[:, i % 8, :], ps_sc[:, i % 2, :], AF.Exp,
                                     scale=ISD).then_inc(exps.h, 1)
                    if 0 <= k - 5 < N:
                        i = k - 5
                        s.wait_ge(mmrb.h, mmrb.at(i + 1))
                        if i >= 4:
                            s.wait_ge(vphat.h, vphat.at(i // 2))
                        s.activation(arbsb[:, i % 4, :], ps_arb[:, i % 2, :], AF.Copy
                                     ).then_inc(sarb.h, 1)
                    if 0 <= k - 9 < N:
                        i = k - 9
                        s.wait_ge(mmav.h, mmav.at(i + 1))
                        s.activation(attn_buf[:, :, 16 * i:16 * i + 16],
                                     ps_av[:, i % 2, :].rearrange("d (t h) -> d h t", t=16),
                                     AF.Copy).then_inc(sattn.h, 1)
                for s2 in (exps, svp, sarb, sattn):
                    s.wait_ge(s2.h, s2.at(N))

            def f_vector(v):
                for w in range(N // 2 + 2):
                    if w < N // 2:
                        for i in (2 * w, 2 * w + 1):
                            v.wait_ge(mmsum.h, mmsum.at(i + 1))
                            if i >= 4:
                                v.wait_ge(mmrb.h, mmrb.at(i - 3))
                            with nc.allow_low_precision(reason="reciprocal feeds a "
                                                        "bf16 product chain"):
                                v.reciprocal(sumb[0:1, i % 4, :], ps_asum[0:1, i % 2, :]
                                             ).then_inc(vrec.h, 1)
                    if 0 <= w - 2 < N // 2:
                        wp = w - 2
                        i0 = 2 * wp
                        v.wait_ge(sarb.h, sarb.at(i0 + 2))
                        v.wait_ge(exps.h, exps.at(i0 + 2))
                        if wp >= 2:
                            v.wait_ge(mmav.h, mmav.at(i0 - 2))
                        v.tensor_mul(phatb[:, wp % 2, :, :],
                                     expb[:, i0 % 8:i0 % 8 + 2, :],
                                     arbsb[:, i0 % 4:i0 % 4 + 2, :]).then_inc(vphat.h, 1)
                v.wait_ge(vrec.h, vrec.at(N))
                v.wait_ge(vphat.h, vphat.at(N // 2))

            run_phase(name, sync=f_sync, tensor=f_tensor, scalar=f_scalar,
                      vector=f_vector, gpsimd=gp_extra)
            for s2 in (mmsc, mmsum, mmrb, mmav, exps, sattn, vrec, sarb):
                s2.adv(N)
            svp.adv(16 * (N // 8))
            vphat.adv(N // 2)

        def ln_scope(stk):
            ps_sum = stk.enter_context(nc.psum_tensor(_u("ps_sum"), [128, TOK], F32))
            ps_sumsq = stk.enter_context(nc.psum_tensor(_u("ps_sumsq"), [128, TOK], F32))
            ps_rb = stk.enter_context(nc.psum_tensor(_u("ps_rb"), [128, TOK], F32))
            ps_mb = stk.enter_context(nc.psum_tensor(_u("ps_mb"), [128, TOK], F32))
            lntmp = stk.enter_context(nc.sbuf_tensor(_u("lntmp"), [128, 2, 4, TOK], BF16))
            lnb = alloc_ln(stk)
            return ps_sum, ps_sumsq, ps_rb, ps_mb, lntmp, lnb

        def ffn_sublayer(tag, src_ln, resid, dst, ln_pre=None):
            with ExitStack() as stF:
                xln = stF.enter_context(nc.sbuf_tensor("xln" + tag, [128, NCH, TOK], BF16))
                w12buf = stF.enter_context(nc.sbuf_tensor("w12b" + tag, [128, 2, NCH, 128], BF16))

                def pf_w12(g):
                    g.dma_start(out=w12buf[:, 0], in_=w12_t[0]).then_inc(wld[0].h, 16)
                    g.dma_start(out=w12buf[:, 1], in_=w12_t[32]).then_inc(wld[1].h, 16)

                with ExitStack() as stL:
                    ps_sum, ps_sumsq, ps_rb, ps_mb, lntmp, lnb = ln_scope(stL)
                    globs.update(ps_sum=ps_sum, ps_sumsq=ps_sumsq, ps_rb=ps_rb,
                                 ps_mb=ps_mb, lntmp=lntmp)
                    ln_phase(tag + "1", src_ln, xln, lnb, pre=ln_pre, gp_extra=pf_w12)
                with ExitStack() as stM:
                    x4 = stM.enter_context(nc.sbuf_tensor("x4" + tag, [128, NCH, TOK], BF16))
                    g1b = stM.enter_context(nc.sbuf_tensor("g1b" + tag, [128, 2, TOK], BF16))
                    x2bb = stM.enter_context(nc.sbuf_tensor("x2b" + tag, [128, 2, TOK], BF16))
                    wbuf3 = stM.enter_context(nc.sbuf_tensor("w3b" + tag, [128, 2, NCH, 128], BF16))

                    def pf_w3(g):
                        g.dma_start(out=wbuf3[:, 0], in_=w3_t[0]).then_inc(wld[0].h, 16)
                        g.dma_start(out=wbuf3[:, 1], in_=w3_t[1]).then_inc(wld[1].h, 16)

                    with ExitStack() as stM2:
                        psA_ = stM2.enter_context(nc.psum_tensor(_u("psA"), [128, 2, TOK], F32))
                        psB_ = stM2.enter_context(nc.psum_tensor(_u("psB"), [128, 2, TOK], F32))
                        globs.update(psA=psA_, psB=psB_)
                        ffn_m_phase(tag + "2", xln, x4, w12buf, g1b, x2bb,
                                    skip_first=2, gp_extra=pf_w3)
                    with ExitStack() as stP:
                        ps3_ = stP.enter_context(nc.psum_tensor(_u("ps3"), [128, 2, TOK], F32))
                        xcb = stP.enter_context(nc.sbuf_tensor("xcb" + tag, [128, 2, TOK], F32))
                        x1b = stP.enter_context(nc.sbuf_tensor("x1b" + tag, [128, 2, TOK], F32))
                        f05b = stP.enter_context(nc.sbuf_tensor("f05" + tag, [128, 2, TOK], F32))
                        globs.update(ps3=ps3_)
                        proj_out_phase(tag + "3", w3_t, x4, b3h, 0.5, resid, dst,
                                       wbuf3, xcb, x1b, f05b, skip_first=2)

        ffn_sublayer("A", xT, xT, x1T, ln_pre=p0_pre)
        p0_adv()

        with ExitStack() as stB:
            q_all = stB.enter_context(nc.sbuf_tensor("q_all", [128, TOK, NT3], BF16))
            k_all = stB.enter_context(nc.sbuf_tensor("k_all", [128, TOK, KVT], BF16))
            v_fm = stB.enter_context(nc.sbuf_tensor("v_fm", [128, TOK, KVT], BF16))
            with ExitStack() as stB1:
                xln2 = stB1.enter_context(nc.sbuf_tensor("xln2", [128, NCH, TOK], BF16))
                with ExitStack() as stL:
                    ps_sum, ps_sumsq, ps_rb, ps_mb, lntmp, lnb = ln_scope(stL)
                    globs.update(ps_sum=ps_sum, ps_sumsq=ps_sumsq, ps_rb=ps_rb,
                                 ps_mb=ps_mb, lntmp=lntmp)
                    ln_phase("B1", x1T, xln2, lnb)
                with ExitStack() as stQ:
                    psq_ = stQ.enter_context(nc.psum_tensor(_u("psq"), [128, 2, TOK], F32))
                    wbufq = stQ.enter_context(nc.sbuf_tensor("wbufq", [128, 2, NCH, 128], BF16))
                    qrot = stQ.enter_context(nc.sbuf_tensor("qrot", [128, 2, TOK], BF16))
                    qpl = stQ.enter_context(nc.sbuf_tensor("qpl", [128, 2, TOK], BF16))
                    rtmp1_ = stQ.enter_context(nc.sbuf_tensor("rtmp1", [128, TOK], BF16))
                    rtmp2_ = stQ.enter_context(nc.sbuf_tensor("rtmp2", [128, TOK], BF16))
                    sinsb_ = stQ.enter_context(nc.sbuf_tensor("sinsb", [128, TOK], F32))
                    cossb_ = stQ.enter_context(nc.sbuf_tensor("cossb", [128, TOK], F32))
                    globs.update(psq=psq_, rtmp1=rtmp1_, rtmp2=rtmp2_,
                                 sinsb=sinsb_, cossb=cossb_)
                    qkv_phase("B2", xln2, q_all, k_all, v_fm, wbufq, qrot, qpl)
            with ExitStack() as stB3:
                attn_buf = stB3.enter_context(nc.sbuf_tensor("attn_buf", [128, NT3, TOK], BF16))
                wbufo = stB3.enter_context(nc.sbuf_tensor("wbufo", [128, 2, NCH, 128], BF16))

                def pf_wo(g):
                    g.dma_start(out=wbufo[:, 0], in_=wo_t[0]).then_inc(wld[0].h, 16)
                    g.dma_start(out=wbufo[:, 1], in_=wo_t[1]).then_inc(wld[1].h, 16)

                with ExitStack() as stAT:
                    ps_sc_ = stAT.enter_context(nc.psum_tensor(_u("ps_sc"), [128, 2, 512], F32))
                    ps_asum_ = stAT.enter_context(nc.psum_tensor(_u("ps_asum"), [128, 2, 512], F32))
                    ps_arb_ = stAT.enter_context(nc.psum_tensor(_u("ps_arb"), [128, 2, 512], F32))
                    ps_av_ = stAT.enter_context(nc.psum_tensor(_u("ps_av"), [128, 2, 512], F32))
                    expb = stAT.enter_context(nc.sbuf_tensor("expb", [128, 8, 512], BF16))
                    phatb = stAT.enter_context(nc.sbuf_tensor("phatb", [128, 2, 2, 512], BF16))
                    vpackb = stAT.enter_context(nc.sbuf_tensor("vpackb", [128, 16, 128], BF16))
                    sumb = stAT.enter_context(nc.sbuf_tensor("sumb", [1, 4, 512], BF16))
                    arbsb = stAT.enter_context(nc.sbuf_tensor("arbsb", [128, 4, 512], BF16))
                    globs.update(ps_sc=ps_sc_, ps_asum=ps_asum_,
                                 ps_arb=ps_arb_, ps_av=ps_av_)
                    attn_phase("B3", q_all, k_all, v_fm, attn_buf,
                               (expb, phatb, vpackb, sumb, arbsb), gp_extra=pf_wo)
                with ExitStack() as stP:
                    ps3_ = stP.enter_context(nc.psum_tensor(_u("ps3"), [128, 2, TOK], F32))
                    xcb = stP.enter_context(nc.sbuf_tensor("xcbB", [128, 2, TOK], F32))
                    x1b = stP.enter_context(nc.sbuf_tensor("x1bB", [128, 2, TOK], F32))
                    f05B = stP.enter_context(nc.sbuf_tensor("f05B", [128, 2, TOK], F32))
                    globs.update(ps3=ps3_)
                    proj_out_phase("B4", wo_t, attn_buf, None, 1.0, x1T, x2T,
                                   wbufo, xcb, x1b, f05B, skip_first=2)

        ffn_sublayer("C", x2T, x2T, outT)

    return nc


_CACHE = {}


def _host_tables():
    half = 64
    i = np.arange(half, dtype=np.float64)
    emb = 10000.0 ** (2.0 * np.floor(i / 2.0) / half)
    pos = np.arange(2048, dtype=np.float64)
    fr = pos[:, None] * emb[None, :]
    import jax
    import jax.numpy as jnp
    with jax.default_device(jax.devices("cpu")[0]):
        i32 = jnp.arange(half, dtype=jnp.float32)
        emb32 = 10000.0 ** (2.0 * jnp.floor(i32 / 2.0) / half)
        pos32 = jnp.arange(2048, dtype=jnp.float32)
        fr32 = pos32[:, None] * emb32[None, :]
        sin_j = np.asarray(jnp.sin(fr32))
        cos_j = np.asarray(jnp.cos(fr32))
    sin_sgn = np.concatenate([-sin_j, sin_j], axis=-1)
    cos_full = np.concatenate([cos_j, cos_j], axis=-1)
    C = float(2 ** 30)
    mA = np.zeros((17, 128), dtype=np.float32)
    mB = np.zeros((17, 512), dtype=np.float32)
    mA[0, :] = 1.0
    mB[0, :] = -C
    for t in range(16):
        mA[1 + t, 8 * t:8 * t + 8] = 1.0
        mB[1 + t, 32 * t:32 * t + 32] = C
    ide = np.eye(128, dtype=np.float32)
    return sin_sgn, cos_full, mA, mB, ide


def kernel(L0_attn_hidden_states, L0_attn_Wq, L0_attn_Wk, L0_attn_Wv, L0_attn_Wo,
           L0_ffn_w12, L0_ffn_b12, L0_ffn_w3, L0_ffn_b3):
    x = np.asarray(L0_attn_hidden_states, dtype=np.float32)
    Wq = np.asarray(L0_attn_Wq, dtype=np.float32)
    Wk = np.asarray(L0_attn_Wk, dtype=np.float32)
    Wv = np.asarray(L0_attn_Wv, dtype=np.float32)
    Wo = np.asarray(L0_attn_Wo, dtype=np.float32)
    w12 = np.asarray(L0_ffn_w12, dtype=np.float32)
    b12 = np.asarray(L0_ffn_b12, dtype=np.float32)
    w3 = np.asarray(L0_ffn_w3, dtype=np.float32)
    b3 = np.asarray(L0_ffn_b3, dtype=np.float32)

    wq_t = np.ascontiguousarray(Wq.reshape(32, 128, 32, 128).transpose(2, 1, 0, 3))
    wk_t = np.ascontiguousarray(Wk.reshape(32, 128, 8, 128).transpose(2, 1, 0, 3))
    wv_t = np.ascontiguousarray(Wv.reshape(32, 128, 8, 128).transpose(2, 1, 0, 3))
    wo_t = np.ascontiguousarray(Wo.reshape(32, 128, 32, 128).transpose(2, 1, 0, 3))
    w12_t = np.ascontiguousarray(w12.reshape(64, 128, 32, 128).transpose(0, 3, 2, 1))
    w3_t = np.ascontiguousarray(w3.reshape(32, 128, 32, 128).transpose(0, 3, 2, 1))
    b12T = np.ascontiguousarray(b12.reshape(64, 128).T)
    b3T = np.ascontiguousarray(b3.reshape(32, 128).T)

    sin_sgn, cos_full, mA, mB, ide = _host_tables()

    xr = x.reshape(4, 2, 1024, 4096)
    in_maps = []
    for core in range(8):
        b, hh = core // 2, core % 2
        xTc = np.ascontiguousarray(xr[b, hh].T)
        p0 = hh * 1024
        in_maps.append({
            "xT": xTc,
            "wq_t": wq_t, "wk_t": wk_t, "wv_t": wv_t, "wo_t": wo_t,
            "w12_t": w12_t, "w3_t": w3_t,
            "b12T": b12T, "b3T": b3T,
            "sinT": np.ascontiguousarray(sin_sgn[p0:p0 + 1024].T),
            "cosT": np.ascontiguousarray(cos_full[p0:p0 + 1024].T),
            "maskA": mA, "maskB": mB, "idem": ide,
        })

    nc = _CACHE.get("nc")
    if nc is None:
        nc = build_program()
        _CACHE["nc"] = nc

    res = run_bass_kernel_spmd(nc, in_maps, core_ids=list(range(8)))

    out = np.empty((4, 2048, 4096), dtype=np.float32)
    for core in range(8):
        b, hh = core // 2, core % 2
        out[b, hh * 1024:(hh + 1) * 1024, :] = res.results[core]["outT"].T
    return out
